# version 04
# baseline (speedup 1.0000x reference)
import math
import contextlib
import numpy as np
import ml_dtypes
import concourse.bass as bass
import concourse.mybir as mybir
from concourse.bass_utils import run_bass_kernel_spmd

F32 = mybir.dt.float32
BF16 = mybir.dt.bfloat16
FP16 = mybir.dt.float16
I32 = mybir.dt.int32
U8 = mybir.dt.uint8
AF = mybir.ActivationFunctionType
ALU = mybir.AluOpType
AX = mybir.AxisListType
bfd = ml_dtypes.bfloat16

NSLOT = 6
BSIDE = [512, 1024, 1536, 2560, 3072, 3584]
OWNED = [1, 2, 4, 5]
SLOT_GROUPS_A = [0, 0, 1, 5, 6, 7]
SLOT_GROUPS_B = [1, 2, 3, 3, 4, 5]


def slot_qb(flavor, s):
    if flavor == 0 and s == 0:
        return 0
    return (SLOT_GROUPS_A if flavor == 0 else SLOT_GROUPS_B)[s] * 512


def host_layout(ins):
    x = np.ascontiguousarray(ins["L0_attn_x"])
    maps = []
    consts = {}
    tri_q_ge_k = (np.arange(128)[None, :] >= np.arange(128)[:, None]).astype(np.float32)
    tri_k_ge_q = (np.arange(128)[:, None] >= np.arange(128)[None, :]).astype(np.float32)
    consts["c_ident"] = np.eye(128, dtype=np.float32).astype(bfd)
    consts["c_tri_diag"] = tri_q_ge_k.astype(bfd)
    consts["c_tri_lead"] = tri_k_ge_q.astype(bfd)
    consts["c_ones_row"] = np.ones((1, 128), dtype=bfd)
    consts["c_ones_col"] = np.ones((128, 1), dtype=bfd)
    hmat = np.zeros((8, 128, 16), dtype=np.float32)
    for dc in range(8):
        hmat[dc, 0:64, 2 * dc] = 1.0
        hmat[dc, 64:128, 2 * dc + 1] = 1.0
    consts["c_hmat"] = hmat.reshape(8 * 128, 16).astype(bfd)
    hkv = np.zeros((2, 128, 4), dtype=np.float32)
    for kc in range(2):
        hkv[kc, 0:64, 2 * kc] = 1.0
        hkv[kc, 64:128, 2 * kc + 1] = 1.0
    consts["c_hmat_kv"] = hkv.reshape(2 * 128, 4).astype(bfd)

    def alibi_slopes(n):
        r = 2.0 ** (-8.0 / n)
        return np.array([r ** (i + 1) for i in range(n)], dtype=np.float64)
    slopes = alibi_slopes(16)
    pp = np.arange(128)[:, None, None]
    dd = (np.arange(8)[None, :, None] - 4) * 128
    consts["c_alibase"] = (slopes[None, None, :] * (dd + pp - 511)).astype(np.float32).reshape(128, 128)

    wconsts = dict(
        w_l0_wq=ins["L0_attn_wq"], w_l0_wk=ins["L0_attn_wk"], w_l0_wv=ins["L0_attn_wv"],
        w_l0_wo=ins["L0_attn_wo"], w_l0_wph=ins["L0_attn_w_phase"],
        w_l0_wg=np.ascontiguousarray(ins["L0_ffn_w_gate"].T).astype(np.int32),
        w_l0_wu=np.ascontiguousarray(ins["L0_ffn_w_up"].T).astype(np.int32),
        w_l0_wd=np.ascontiguousarray(ins["L0_ffn_w_down"].T).astype(np.int32),
        w_l1_wq=np.ascontiguousarray(ins["L1_attn_wq"].T),
        w_l1_wk=np.ascontiguousarray(ins["L1_attn_wk"].T),
        w_l1_wv=np.ascontiguousarray(ins["L1_attn_wv"].T),
        w_l1_wo=np.ascontiguousarray(ins["L1_attn_wo"].T),
        w_l1_w1=np.ascontiguousarray(ins["L1_ffn_w1"].T),
        w_l1_w2=np.ascontiguousarray(ins["L1_ffn_w2"].T),
        v_carrier=np.ascontiguousarray(ins["L0_attn_carrier"].reshape(16, 1)),
        v_lock=np.ascontiguousarray(ins["L0_attn_lock"].reshape(16, 1)),
        v_bw=np.ascontiguousarray(ins["L0_attn_bandwidth"].reshape(16, 1)),
        v_amask=np.ascontiguousarray(ins["L0_attn_active_mask"]).astype(np.uint8).reshape(16, 1),
        v_qnw=np.ascontiguousarray(ins["L1_attn_q_norm_w"].reshape(1024, 1)),
        v_knw=np.ascontiguousarray(ins["L1_attn_k_norm_w"].reshape(256, 1)),
        v_lng=np.ascontiguousarray(ins["L1_ffn_ln_g"].reshape(1024, 1)),
        v_lnb=np.ascontiguousarray(ins["L1_ffn_ln_b"].reshape(1024, 1)),
    )
    def _b16r(x):
        return np.asarray(x, dtype=np.float32).astype(bfd).astype(np.float64)
    OWNED6 = [1, 2, 4, 5]
    qspec = np.zeros((40, 8192), np.float64)
    for oix in range(4):
        base = oix * 10
        qspec[base + 0] = -1e30
        for h in range(16):
            sl = np.float64(np.float32(slopes[h]))
            shi = _b16r(sl); slo = _b16r(sl - shi); sll = _b16r(sl - shi - slo)
            cols = slice(h * 512, (h + 1) * 512)
            qspec[base + 1, cols] = shi; qspec[base + 2, cols] = slo
            qspec[base + 3, cols] = sll
            qspec[base + 4, cols] = shi; qspec[base + 5, cols] = slo
            qspec[base + 6, cols] = sll
            jj = np.arange(512, dtype=np.float64)
            W = -8.0 * sl * (jj + OWNED6[oix] * 512)
            Whi = _b16r(W); Wlo = _b16r(W - Whi); Wll = _b16r(W - Whi - Wlo)
            qspec[base + 7, cols] = Whi; qspec[base + 8, cols] = Wlo
            qspec[base + 9, cols] = Wll
    wconsts["c_qspec"] = qspec.astype(np.float32).astype(bfd)
    cidx = (np.arange(3072) // 128).astype(np.float64)
    pidx = (np.arange(3072) % 128).astype(np.float64)

    for c in range(8):
        b = c // 2
        flavor = c % 2
        groups = SLOT_GROUPS_A if flavor == 0 else SLOT_GROUPS_B
        xb = np.ascontiguousarray(x[b].T)
        xg = np.ascontiguousarray(
            np.concatenate([x[b, g * 512:(g + 1) * 512, :].T for g in groups], axis=1))
        sm0 = np.zeros((6, 28), dtype=np.float32)
        for s in range(6):
            nb = BSIDE[s] // 128
            qbc = slot_qb(flavor, s) // 128
            for ch in range(nb):
                if ch >= qbc:
                    sm0[s, ch] = -1e30
        sm1 = np.zeros((4, 8), dtype=np.float32)
        if flavor == 0:
            sm1[0, 0:4] = -1e30
        kspec = np.zeros((12, 3072), np.float32)
        if flavor == 0:
            kspec[0, 0:512] = 1.0
        kspec[1:4] = (1024.0 * cidx)[None, :]
        kspec[4:7] = (8.0 * (pidx + 1.0))[None, :]
        kspec[7:10] = 1.0
        m = dict(consts)
        m.update(wconsts)
        m.update(c_kspec=kspec.astype(bfd))
        m.update(xb=xb, xg=xg,
                 sm0=np.repeat(sm0.reshape(1, 168), 128, axis=0).astype(np.float32),
                 sm0row=sm0.reshape(1, 168).astype(bfd),
                 sm1=np.repeat(sm1.reshape(1, 32), 128, axis=0).astype(np.float32))
        maps.append(m)
    return maps


class P:
    def __init__(self, nc):
        self.nc = nc
        self.q = {k: [] for k in ("sync", "vector", "scalar", "tensor", "gpsimd")}
        self.sem = {}
        self.semct = {}
        self.stack = contextlib.ExitStack()

    def new_sem(self, name):
        h = self.stack.enter_context(self.nc.semaphore(name))
        self.sem[name] = h
        self.semct[name] = 0

    def W(self, eng, *sems):
        t = tuple((self.sem[s], self.semct[s]) for s in sems)
        self.q[eng].append(lambda e, t=t: [e.wait_ge(h, v) for h, v in t if v > 0])

    def Wv(self, eng, sem, val):
        if val > 0:
            self.q[eng].append(lambda e, h=self.sem[sem], v=val: e.wait_ge(h, v))

    def O(self, eng, fn, inc=None, k=1):
        if inc is not None:
            self.semct[inc] += k
            h = self.sem[inc]
            self.q[eng].append(lambda e, fn=fn, h=h, k=k: fn(e).then_inc(h, k))
        else:
            self.q[eng].append(fn)

    def ct(self, sem):
        return self.semct[sem]

    def sbuf(self, name, shape, dtype):
        return self.stack.enter_context(self.nc.sbuf_tensor(name, shape, dtype))

    def psum(self, name, shape, dtype=F32):
        return self.stack.enter_context(self.nc.psum_tensor(name, shape, dtype))

    def finalize(self):
        nc = self.nc
        with nc.Block() as block:
            @block.sync
            def _(sync):
                for f in self.q["sync"]:
                    f(sync)
            @block.vector
            def _(vector):
                for f in self.q["vector"]:
                    f(vector)
            @block.scalar
            def _(scalar):
                for f in self.q["scalar"]:
                    f(scalar)
            @block.tensor
            def _(tensor):
                for f in self.q["tensor"]:
                    f(tensor)
            @block.gpsimd
            def _(gpsimd):
                for f in self.q["gpsimd"]:
                    f(gpsimd)
        self.stack.close()
        return nc


def build(stage="l0", dbg=(), nranks=8):
    nc = bass.Bass("TRN2", target_bir_lowering=False, debug=False, detect_race_conditions=False)
    p = P(nc)
    g = {}

    def dram_in(name, shape, dtype=F32):
        g[name] = nc.dram_tensor(name, shape, dtype, kind="ExternalInput").ap()

    def dram_out(name, shape, dtype=F32):
        g[name] = nc.dram_tensor(name, shape, dtype, kind="ExternalOutput").ap()

    def dram_int(name, shape, dtype=F32):
        g[name] = nc.dram_tensor(name, shape, dtype).ap()

    dram_in("xb", [1024, 4096]); dram_in("xg", [1024, 3072])
    dram_in("sm0", [128, 168]); dram_in("sm1", [128, 32])
    dram_in("sm0row", [1, 168], BF16)
    dram_in("c_ident", [128, 128], BF16)
    dram_in("c_tri_diag", [128, 128], BF16); dram_in("c_tri_lead", [128, 128], BF16)
    dram_in("c_ones_row", [1, 128], BF16); dram_in("c_ones_col", [128, 1], BF16)
    dram_in("c_hmat", [1024, 16], BF16); dram_in("c_hmat_kv", [256, 4], BF16)
    dram_in("c_alibase", [128, 128])
    dram_in("w_l0_wq", [1024, 1024]); dram_in("w_l0_wk", [1024, 1024])
    dram_in("w_l0_wv", [1024, 1024]); dram_in("w_l0_wo", [1024, 1024])
    dram_in("w_l0_wph", [1024, 2])
    dram_in("w_l0_wg", [1024, 3072], I32); dram_in("w_l0_wu", [1024, 3072], I32)
    dram_in("w_l0_wd", [3072, 1024], I32)
    dram_in("w_l1_wq", [1024, 1024]); dram_in("w_l1_wk", [1024, 256])
    dram_in("w_l1_wv", [1024, 256]); dram_in("w_l1_wo", [1024, 1024])
    dram_in("w_l1_w1", [1024, 4096]); dram_in("w_l1_w2", [4096, 1024])
    for n in ("v_carrier", "v_lock", "v_bw"):
        dram_in(n, [16, 1])
    dram_in("v_amask", [16, 1], U8)
    dram_in("v_qnw", [1024, 1]); dram_in("v_knw", [256, 1])
    dram_in("v_lng", [1024, 1]); dram_in("v_lnb", [1024, 1])
    dram_in("c_kspec", [12, 3072], BF16)
    dram_in("c_qspec", [40, 8192], BF16)

    dram_int("hK", [128, 8 * 3584], BF16)
    dram_int("hVg", [128, 28 * 1040], BF16)
    dram_int("hS0", [128, 8 * 3072], BF16)
    dram_int("hG0", [128, 8 * 3072], BF16)
    dram_int("hX1", [128, 8 * 3072], F32)
    dram_int("hS1", [128, 8 * 2048], BF16)
    dram_int("hH1", [128, 32 * 512], F32)
    dram_int("hH2", [128, 32 * 2048], F32)
    dram_int("hLG", [16, 4096])
    dram_int("hXN", [128, 6 * 4096], BF16)
    dram_int("hH0", [128, 24 * 6 * 512], BF16)
    dram_int("hGT", [16, 4096])
    for qq in (1, 2):
        for r in range(16):
            dram_int(f"cc{qq}i{r}", [1, 1], F32)
            dram_int(f"cc{qq}o{r}", [1, 1], F32)
    for dn, shp, dt in dbg:
        dram_out(dn, shp, dt)
    if stage == "full":
        dram_out("y", [1024, 2048], F32)

    sb = {}
    def S_(name, shape, dtype=F32):
        sb[name] = p.sbuf(name, shape, dtype)
        return sb[name]

    S_("ident", [128, 128], BF16)
    S_("tri_diag", [128, 128], BF16); S_("tri_lead", [128, 128], BF16)
    S_("ones_row", [1, 128], BF16); S_("ones_col", [128, 1], BF16)
    S_("hmat", [128, 8 * 16], BF16)
    S_("carrier", [16, 1]); S_("lock", [16, 1]); S_("bwv", [16, 1])
    S_("amask8", [16, 1], U8); S_("amaskf", [16, 1])
    S_("sinc", [16, 1]); S_("cosc", [16, 1]); S_("invbw", [16, 1]); S_("lock1m", [16, 1])
    S_("sv", [128, 16])
    S_("svi", [128, 1], I32)
    S_("epsc", [128, 4])
    S_("wk16", [128, 8 * 1024], BF16)
    S_("wv16", [128, 8 * 1024], BF16)
    S_("wph16", [128, 16], BF16)
    S_("xchunk", [128, 8 * 512])
    S_("row", [1, 2048])
    S_("rowb", [1, 2048], BF16)
    S_("t0", [128, 512]); S_("t1", [128, 512]); S_("t2", [128, 512])
    S_("xnb", [128, 8 * 512], BF16)
    S_("kstage", [128, 8 * 512], BF16)
    S_("vgstage", [128, 4 * 1040], BF16)
    S_("gr0", [16, 512]); S_("gr1", [16, 512]); S_("gr2", [16, 512])
    S_("gatec", [16, 512])
    S_("lngate_tm", [128, 28 * 16])
    S_("nrm", [16, 8])
    S_("xn_slots", [128, 6 * 4 * 512], BF16)
    S_("qhat", [128, 8 * 512], BF16)
    S_("wq16", [128, 8 * 1024], BF16)
    S_("wo16", [128, 8 * 1024], BF16)
    S_("biasS", [128, 4 * 16])
    S_("m0rep", [128, 16])
    S_("m0b", [16, 1], BF16)
    S_("amrow", [1, 16])
    S_("kst16", [128, 1024], BF16)
    S_("smrowb", [1, 168], BF16)
    S_("wstr", [128, 2048], BF16)
    S_("qnw8", [128, 8]); S_("knw2", [128, 2]); S_("lng8", [128, 8]); S_("lnb8", [128, 8])
    S_("v1g", [128, 24 * 260], BF16)
    S_("onesf", [1, 128]); S_("onescf", [128, 1])
    S_("lgslot", [128, 4 * 16])
    S_("vgbuf", [128, 2 * 1040], BF16)
    sb["kbuf"] = None
    S_("ptil", [128, 8 * 512], BF16)
    S_("bias0", [128, 28 * 16])
    S_("obuf", [128, 8 * 512], BF16)

    psALL = p.psum("psALL", [128, 4096])
    psA = psALL[:, 0:512]
    psB = psALL[:, 512:1024]
    psK = psALL[:, 1024:3072]
    psSC = psALL[:, 0:3072]
    psAV = psALL[:, 3072:4096]
    psC = psALL[:, 3072:3584]

    kbuf_ap = sb["xchunk"][:].bitcast(BF16)
    sb["hab_a"] = sb["vgbuf"][:].bitcast(FP16)[:, 0:512]
    sb["hab_b"] = sb["vgbuf"][:].bitcast(FP16)[:, 512:1024]
    sb["hti"] = sb["vgbuf"][:, 1024:1536]
    for s in ("ld", "wld", "vd", "sd", "td", "gd", "sp", "cc", "fin", "pr"):
        p.new_sem(s)

    const_loads = [("ident", "c_ident"), ("tri_diag", "c_tri_diag"),
                   ("tri_lead", "c_tri_lead"), ("ones_row", "c_ones_row"),
                   ("ones_col", "c_ones_col"),
                   ("smrowb", "sm0row"), ("carrier", "v_carrier"),
                   ("lock", "v_lock"), ("bwv", "v_bw"), ("amask8", "v_amask")]
    for dst, src in const_loads:
        p.O("sync", lambda e, d=dst, s=src: e.dma_start(out=sb[d][:], in_=g[s]), inc="ld", k=16)
    for i in range(8):
        p.O("sync", lambda e, i=i: e.dma_start(out=sb["hmat"][:, i * 16:(i + 1) * 16],
            in_=g["c_hmat"][i * 128:(i + 1) * 128, :]), inc="ld", k=16)

    def cast_weight(wname, dst, cols, colstep=512):
        nh = cols // colstep
        for h in range(nh):
            p.W("sync", "vd")
            for i in range(8):
                p.O("sync", lambda e, i=i, h=h, w=wname, cs=colstep: e.dma_start(
                    out=sb["xchunk"][:, i * cs:(i + 1) * cs],
                    in_=g[w][i * 128:(i + 1) * 128, h * cs:(h + 1) * cs]), inc="wld", k=16)
            p.W("vector", "wld")
            for i in range(8):
                p.O("vector", lambda e, i=i, h=h, d=dst, cols=cols, cs=colstep: e.tensor_copy(
                    out=sb[d][:, i * cols + h * cs: i * cols + h * cs + cs],
                    in_=sb["xchunk"][:, i * cs:(i + 1) * cs]), inc="vd")

    cast_weight("w_l0_wk", "wk16", 1024)
    cast_weight("w_l0_wv", "wv16", 1024)
    cast_weight("w_l0_wq", "wq16", 1024)
    cast_weight("w_l0_wo", "wo16", 1024)
    p.W("sync", "vd")
    for i in range(8):
        p.O("sync", lambda e, i=i: e.dma_start(
            out=sb["xchunk"][:, i * 2:(i + 1) * 2],
            in_=g["w_l0_wph"][i * 128:(i + 1) * 128, :]), inc="wld", k=16)
    p.W("vector", "wld")
    for i in range(8):
        p.O("vector", lambda e, i=i: e.tensor_copy(
            out=sb["wph16"][:, i * 2:(i + 1) * 2],
            in_=sb["xchunk"][:, i * 2:(i + 1) * 2]), inc="vd")

    def epsinit(e):
        e.memset(sb["epsc"][:, 0:1], 1e-5)
        e.memset(sb["epsc"][:, 1:2], 1e-30)
        e.memset(sb["epsc"][:, 2:3], 1e-6)
        e.memset(sb["epsc"][:, 3:4], 0.0)
        return e.memset(sb["sv"][:], 0.0)
    p.O("gpsimd", epsinit, inc="gd")
    p.W("vector", "gd")
    p.W("scalar", "gd")
    p.W("vector", "ld")
    def VCH(*fns):
        for fn in fns:
            p.O("vector", fn, inc="vd")
            p.Wv("vector", "vd", p.ct("vd"))
    VCH(lambda e: e.tensor_copy(out=sb["amaskf"][:], in_=sb["amask8"][:]),
        lambda e: e.tensor_scalar(out=sb["sv"][0:16, 0:1], in0=sb["carrier"][:],
                                  scalar1=1.0 / (2 * math.pi), scalar2=None, op0=ALU.mult),
        lambda e: e.tensor_scalar(out=sb["sv"][0:16, 1:2], in0=sb["sv"][0:16, 0:1],
                                  scalar1=0.5, scalar2=None, op0=ALU.is_ge),
        lambda e: e.tensor_scalar(out=sb["sv"][0:16, 2:3], in0=sb["sv"][0:16, 0:1],
                                  scalar1=1.5, scalar2=None, op0=ALU.is_ge),
        lambda e: e.tensor_tensor(out=sb["sv"][0:16, 2:3], in0=sb["sv"][0:16, 1:2],
                                  in1=sb["sv"][0:16, 2:3], op=ALU.add),
        lambda e: e.scalar_tensor_tensor(out=sb["sv"][0:16, 3:4], in0=sb["sv"][0:16, 2:3],
                                         scalar=-2 * math.pi, in1=sb["carrier"][:],
                                         op0=ALU.mult, op1=ALU.add),
        lambda e: e.tensor_scalar(out=sb["lock1m"][:], in0=sb["lock"][:], scalar1=-1.0,
                                  scalar2=1.0 + 1e-6, op0=ALU.mult, op1=ALU.add))
    p.O("vector", lambda e: e.reciprocal(out=sb["invbw"][:], in_=sb["bwv"][:]), inc="vd")
    p.Wv("vector", "vd", p.ct("vd"))
    p.W("scalar", "vd")
    p.O("scalar", lambda e: e.activation(out=sb["sinc"][:], in_=sb["sv"][0:16, 3:4],
                                         func=AF.Sin), inc="sd")
    p.W("vector", "sd")
    VCH(lambda e: e.tensor_tensor(out=sb["sv"][0:16, 5:6], in0=sb["sinc"][:],
                                  in1=sb["sinc"][:], op=ALU.mult),
        lambda e: e.tensor_scalar(out=sb["sv"][0:16, 6:7], in0=sb["sv"][0:16, 5:6],
                                  scalar1=-1.0, scalar2=1.0, op0=ALU.mult, op1=ALU.add))
    p.W("scalar", "vd")
    p.O("scalar", lambda e: e.activation(out=sb["sv"][0:16, 7:8], in_=sb["sv"][0:16, 6:7],
                                         func=AF.Sqrt), inc="sd")
    p.W("vector", "sd")
    VCH(lambda e: e.tensor_scalar(out=sb["sv"][0:16, 8:9], in0=sb["sv"][0:16, 3:4],
                                  scalar1=-1.0, scalar2=None, op0=ALU.mult),
        lambda e: e.tensor_tensor(out=sb["sv"][0:16, 8:9], in0=sb["sv"][0:16, 3:4],
                                  in1=sb["sv"][0:16, 8:9], op=ALU.max),
        lambda e: e.tensor_scalar(out=sb["sv"][0:16, 9:10], in0=sb["sv"][0:16, 8:9],
                                  scalar1=math.pi / 2, scalar2=None, op0=ALU.is_gt),
        lambda e: e.tensor_scalar(out=sb["sv"][0:16, 9:10], in0=sb["sv"][0:16, 9:10],
                                  scalar1=-2.0, scalar2=1.0, op0=ALU.mult, op1=ALU.add),
        lambda e: e.tensor_tensor(out=sb["cosc"][:], in0=sb["sv"][0:16, 7:8],
                                  in1=sb["sv"][0:16, 9:10], op=ALU.mult),
        lambda e: e.memset(sb["nrm"][:], 0.0))
    p.W("vector", "vd")
    p.O("vector", lambda e: e.tensor_copy(out=sb["m0b"][0:16, 0:1], in_=sb["amaskf"][:]), inc="vd")
    p.W("tensor", "vd")
    p.O("tensor", lambda e: e.transpose(psK.bitcast(BF16)[0:1, 0:16],
        sb["m0b"][0:16, 0:1], sb["ident"][0:16, 0:16]), inc="td")
    p.W("vector", "td")
    p.O("vector", lambda e: e.tensor_copy(out=sb["amrow"][0:1, 0:16],
        in_=psK.bitcast(BF16)[0:1, 0:16]), inc="vd")

    def emit_ln_chunk(src_ap_fn, xn_dst_fn, f32_dst_fn=None):
        for i in range(8):
            p.O("vector", lambda e, i=i: e.tensor_copy(
                out=sb["qhat"][:, i * 512:(i + 1) * 512], in_=src_ap_fn(i)))
        for i in range(8):
            p.O("vector", lambda e, i=i: e.tensor_tensor(
                out=sb["kstage"][:, i * 512:(i + 1) * 512],
                in0=src_ap_fn(i), in1=src_ap_fn(i), op=ALU.mult), inc="vd" if i == 7 else None)
        p.W("tensor", "vd")
        for i in range(8):
            p.O("tensor", lambda e, i=i: e.matmul(psA[0:1, :], sb["ones_col"][:],
                sb["qhat"][:, i * 512:(i + 1) * 512], start=(i == 0), stop=(i == 7)))
        for i in range(8):
            p.O("tensor", lambda e, i=i: e.matmul(psB[0:1, :], sb["ones_col"][:],
                sb["kstage"][:, i * 512:(i + 1) * 512], start=(i == 0), stop=(i == 7)),
                inc="td" if i == 7 else None)
        p.W("vector", "td")
        def stats(e):
            e.tensor_scalar(out=sb["row"][0:1, 0:512], in0=psA[0:1, :], scalar1=1.0 / 1024.0,
                            scalar2=None, op0=ALU.mult)
            e.tensor_scalar(out=sb["row"][0:1, 512:1024], in0=psB[0:1, :], scalar1=1.0 / 1024.0,
                            scalar2=None, op0=ALU.mult)
            e.tensor_tensor(out=sb["row"][0:1, 1024:1536], in0=sb["row"][0:1, 0:512],
                            in1=sb["row"][0:1, 0:512], op=ALU.mult)
            return e.tensor_tensor(out=sb["row"][0:1, 512:1024], in0=sb["row"][0:1, 512:1024],
                                   in1=sb["row"][0:1, 1024:1536], op=ALU.subtract)
        p.O("vector", stats, inc="vd")
        p.W("scalar", "vd")
        p.O("scalar", lambda e: e.activation(out=sb["row"][0:1, 1024:1536],
                                             in_=sb["row"][0:1, 512:1024],
                                             func=AF.Sqrt, bias=sb['epsc'][0:1, 0:1]), inc="sd")
        p.W("vector", "sd")
        p.O("vector", lambda e: e.reciprocal(out=sb["row"][0:1, 512:1024],
            in_=sb["row"][0:1, 1024:1536]), inc="vd")
        p.Wv("vector", "vd", p.ct("vd"))
        def hilo(e):
            e.tensor_copy(out=sb["rowb"][0:1, 0:512], in_=sb["row"][0:1, 0:512])
            e.tensor_copy(out=sb["row"][0:1, 1536:2048], in_=sb["rowb"][0:1, 0:512])
            e.tensor_tensor(out=sb["rowb"][0:1, 512:1024], in0=sb["row"][0:1, 0:512],
                            in1=sb["row"][0:1, 1536:2048], op=ALU.subtract)
            e.tensor_copy(out=sb["rowb"][0:1, 1024:1536], in_=sb["row"][0:1, 512:1024])
            e.tensor_copy(out=sb["row"][0:1, 1536:2048], in_=sb["rowb"][0:1, 1024:1536])
            return e.tensor_tensor(out=sb["rowb"][0:1, 1536:2048], in0=sb["row"][0:1, 512:1024],
                                   in1=sb["row"][0:1, 1536:2048], op=ALU.subtract)
        p.O("vector", hilo, inc="vd")
        p.W("tensor", "vd")
        for j in range(2):
            p.O("tensor", lambda e, j=j: e.matmul((psA if j == 0 else psB)[0:128, :],
                sb["ones_row"][:], sb["rowb"][0:1, j * 1024:j * 1024 + 512],
                start=True, stop=False))
            p.O("tensor", lambda e, j=j: e.matmul((psA if j == 0 else psB)[0:128, :],
                sb["ones_row"][:], sb["rowb"][0:1, j * 1024 + 512:j * 1024 + 1024],
                start=False, stop=True), inc="td" if j == 1 else None)
        p.W("vector", "td")
        for i in range(8):
            def nrmz(e, i=i):
                e.tensor_tensor(out=sb["t2"][:], in0=src_ap_fn(i), in1=psA[0:128, :],
                                op=ALU.subtract)
                if f32_dst_fn is not None:
                    e.tensor_tensor(out=f32_dst_fn(i), in0=sb["t2"][:], in1=psB[0:128, :],
                                    op=ALU.mult)
                    return e.tensor_copy(out=xn_dst_fn(i), in_=f32_dst_fn(i))
                return e.tensor_tensor(out=xn_dst_fn(i), in0=sb["t2"][:], in1=psB[0:128, :],
                                       op=ALU.mult)
            p.O("vector", nrmz, inc="vd" if i == 7 else None)

    NB = 7
    psNA = psALL[:, 2048:2560]
    psNB = psALL[:, 2560:3072]
    PADP = []
    for s_ in range(NSLOT):
        nb_ = BSIDE[s_] // 128
        mn = min(slot_qb(0, s_), slot_qb(1, s_)) // 128
        PADP.append([c >= mn for c in range(nb_)])

    def emit_gate_chain(gtm_dst):
        def ph_hilo(e):
            e.tensor_copy(out=sb["row"][0:1, 0:512], in_=psA[0:1, :])
            e.tensor_copy(out=sb["row"][0:1, 512:1024], in_=psB[0:1, :])
            for r in range(2):
                e.tensor_copy(out=sb["rowb"][0:1, r * 1024:r * 1024 + 512],
                              in_=sb["row"][0:1, r * 512:(r + 1) * 512])
                e.tensor_copy(out=sb["row"][0:1, 1024 + r * 512:1024 + (r + 1) * 512],
                              in_=sb["rowb"][0:1, r * 1024:r * 1024 + 512])
                e.tensor_tensor(out=sb["rowb"][0:1, r * 1024 + 512:r * 1024 + 1024],
                                in0=sb["row"][0:1, r * 512:(r + 1) * 512],
                                in1=sb["row"][0:1, 1024 + r * 512:1024 + (r + 1) * 512],
                                op=ALU.subtract)
            return e.tensor_copy(out=sb["row"][0:1, 0:1], in_=sb["row"][0:1, 0:1])
        p.O("vector", ph_hilo, inc="vd")
        p.W("tensor", "vd")
        for r in range(2):
            p.O("tensor", lambda e, r=r: e.matmul(
                (psB if r == 0 else psC)[0:16, :], sb["ones_row"][0:1, 0:16],
                sb["rowb"][0:1, r * 1024:r * 1024 + 512], start=True, stop=False))
            p.O("tensor", lambda e, r=r: e.matmul(
                (psB if r == 0 else psC)[0:16, :], sb["ones_row"][0:1, 0:16],
                sb["rowb"][0:1, r * 1024 + 512:r * 1024 + 1024], start=False, stop=True),
                inc="td" if r == 1 else None)
        p.W("vector", "td")
        def gatechain(e):
            e.tensor_copy(out=sb["t1"][0:16, :], in_=psB[0:16, :])
            e.tensor_copy(out=sb["t2"][0:16, :], in_=psC[0:16, :])
            e.tensor_copy(out=sb["gr0"][:], in_=sb["t1"][0:16, :])
            e.tensor_copy(out=sb["gr1"][:], in_=sb["t2"][0:16, :])
            e.tensor_scalar(out=sb["gr2"][:], in0=sb["gr1"][:], scalar1=sb["cosc"][:],
                            scalar2=None, op0=ALU.mult)
            e.scalar_tensor_tensor(out=sb["gr2"][:], in0=sb["gr0"][:],
                                   scalar=sb["sinc"][:], in1=sb["gr2"][:],
                                   op0=ALU.mult, op1=ALU.add)
            e.tensor_tensor(out=sb["gr0"][:], in0=sb["gr0"][:], in1=sb["gr0"][:], op=ALU.mult)
            e.tensor_tensor(out=sb["gr1"][:], in0=sb["gr1"][:], in1=sb["gr1"][:], op=ALU.mult)
            return e.tensor_tensor(out=sb["gr0"][:], in0=sb["gr0"][:], in1=sb["gr1"][:], op=ALU.add)
        p.O("vector", gatechain, inc="vd")
        p.W("scalar", "vd")
        p.O("scalar", lambda e: e.activation(out=sb["gr1"][:], in_=sb["gr0"][:],
                                             func=AF.Sqrt, bias=sb['epsc'][0:16, 1:2]), inc="sd")
        p.W("vector", "sd")
        p.O("vector", lambda e: e.reciprocal(out=sb["gr0"][:], in_=sb["gr1"][:]), inc="vd")
        p.Wv("vector", "vd", p.ct("vd"))
        def gate2(e):
            e.tensor_tensor(out=sb["gr1"][:], in0=sb["gr2"][:], in1=sb["gr0"][:], op=ALU.mult)
            return e.tensor_scalar(out=sb["sv"][0:16, 11:12], in0=sb["invbw"][:],
                                   scalar1=-1.0, scalar2=None, op0=ALU.mult)
        p.O("vector", gate2, inc="vd")
        p.W("scalar", "vd")
        p.O("scalar", lambda e: e.activation(out=sb["gr2"][:], in_=sb["gr1"][:], func=AF.Exp,
                                             scale=sb["invbw"][:], bias=sb["sv"][0:16, 11:12]),
            inc="sd")
        p.W("vector", "sd")
        def gatefin(e):
            e.tensor_scalar(out=sb["gatec"][:], in0=sb["gr2"][:], scalar1=sb["lock"][:],
                            scalar2=sb["lock1m"][:], op0=ALU.mult, op1=ALU.add)
            e.tensor_copy(out=sb["kst16"][0:16, 0:512], in_=sb["gatec"][:])
            e.tensor_copy(out=sb["gr1"][:], in_=sb["kst16"][0:16, 0:512])
            return e.tensor_tensor(out=sb["kst16"][0:16, 512:1024], in0=sb["gatec"][:],
                                   in1=sb["gr1"][:], op=ALU.subtract)
        p.O("vector", gatefin, inc="vd")
        p.W("tensor", "vd")
        for t in range(4):
            p.O("tensor", lambda e, t=t: e.transpose(
                psK.bitcast(BF16)[0:128, t * 32:t * 32 + 16],
                sb["kst16"][0:16, t * 128:(t + 1) * 128], sb["ident"][0:16, 0:16]))
            p.O("tensor", lambda e, t=t: e.transpose(
                psK.bitcast(BF16)[0:128, t * 32 + 16:t * 32 + 32],
                sb["kst16"][0:16, 512 + t * 128: 512 + (t + 1) * 128],
                sb["ident"][0:16, 0:16]), inc="td" if t == 3 else None)
        p.W("vector", "td")
        def combine_gt(e, gd=gtm_dst):
            for t in range(4):
                e.tensor_copy(out=gd(t), in_=psK.bitcast(BF16)[0:128, t * 32:t * 32 + 16])
                e.tensor_copy(out=sb["t0"][:, t * 16:(t + 1) * 16],
                              in_=psK.bitcast(BF16)[0:128, t * 32 + 16:t * 32 + 32])
            for t in range(4):
                e.tensor_tensor(out=gd(t), in0=gd(t),
                                in1=sb["t0"][:, t * 16:(t + 1) * 16], op=ALU.add)
            return e.tensor_copy(out=sb["gr1"][0:1, 0:1], in_=sb["gr0"][0:1, 0:1])
        p.O("vector", combine_gt, inc="vd")

    def emit_ph_proj():
        p.W("tensor", "vd")
        for i in range(8):
            p.O("tensor", lambda e, i=i: e.matmul(
                psA[0:1, :], sb["wph16"][:, i * 2:i * 2 + 1],
                sb["xnb"][:, i * 512:(i + 1) * 512],
                start=(i == 0), stop=(i == 7)))
        for i in range(8):
            p.O("tensor", lambda e, i=i: e.matmul(
                psB[0:1, :], sb["wph16"][:, i * 2 + 1:i * 2 + 2],
                sb["xnb"][:, i * 512:(i + 1) * 512],
                start=(i == 0), stop=(i == 7)), inc="td" if i == 7 else None)
        p.W("vector", "td")

    def emit_kproj(wname, dst, spill=None):
        p.W("tensor", "vd")
        for w in range(2):
            for dt_ in range(4):
                do = w * 4 + dt_
                for i in range(8):
                    p.O("tensor", lambda e, do=do, dt_=dt_, i=i, wn=wname: e.matmul(
                        psK[:, dt_ * 512:dt_ * 512 + 512],
                        sb[wn][:, i * 1024 + do * 128: i * 1024 + do * 128 + 128],
                        sb["xnb"][:, i * 512:(i + 1) * 512],
                        start=(i == 0), stop=(i == 7)), inc="td" if i == 7 else None)
            p.W("vector", "td")
            for dt_ in range(4):
                do = w * 4 + dt_
                p.O("vector", lambda e, do=do, dt_=dt_, d=dst: e.tensor_copy(
                    out=sb[d][:, do * 512:(do + 1) * 512],
                    in_=psK[:, dt_ * 512:dt_ * 512 + 512]), inc="vd")
            p.W("tensor", "vd")
        if spill is not None:
            p.W("sync", "vd")
            for i in range(8):
                p.O("sync", lambda e, i=i, sp=spill: e.dma_start(
                    out=sp(i), in_=sb[dst][:, i * 512:(i + 1) * 512]), inc="sp", k=16)

    def emit_headnorm2(src, outcol):
        for i in range(8):
            p.O("vector", lambda e, i=i, sn=src: e.tensor_tensor(
                out=sb["ptil"][:, i * 512:(i + 1) * 512],
                in0=sb[sn][:, i * 512:(i + 1) * 512],
                in1=sb[sn][:, i * 512:(i + 1) * 512], op=ALU.mult),
                inc="vd" if i == 7 else None)
        p.W("tensor", "vd")
        for i in range(8):
            p.O("tensor", lambda e, i=i: e.matmul(psC[0:16, :],
                sb["hmat"][:, i * 16:(i + 1) * 16],
                sb["ptil"][:, i * 512:(i + 1) * 512],
                start=(i == 0), stop=(i == 7)), inc="td" if i == 7 else None)
        p.W("vector", "td")
        p.O("vector", lambda e, oc=outcol: e.tensor_reduce(
            out=sb["sv"][0:16, oc:oc + 1], in_=psC[0:16, :], axis=AX.X, op=ALU.max), inc="vd")

    def emit_vproj_gated(gtm_src):
        p.W("tensor", "vd")
        for tsub in range(4):
            for w in range(2):
                for i in range(8):
                    p.O("tensor", lambda e, tsub=tsub, w=w, i=i: e.matmul(
                        psK[:, w * 512:(w + 1) * 512],
                        sb["xnb"][:, i * 512 + tsub * 128: i * 512 + tsub * 128 + 128],
                        sb["wv16"][:, i * 1024 + w * 512: i * 1024 + (w + 1) * 512],
                        start=(i == 0), stop=(i == 7)),
                        inc="td" if (i == 7 and w == 1) else None)
            p.W("vector", "td")
            def vevac(e, tsub=tsub, gs=gtm_src):
                gt = gs(tsub)
                for w in range(2):
                    for h8 in range(8):
                        h = w * 8 + h8
                        e.tensor_tensor(
                            out=sb["vgstage"][:, tsub * 1040 + h * 65: tsub * 1040 + h * 65 + 64],
                            in0=psK[:, w * 512 + h8 * 64: w * 512 + (h8 + 1) * 64],
                            in1=gt[:, h8 + w * 8:h8 + w * 8 + 1].broadcast_to([128, 64])
                            if False else gt[:, h:h + 1].broadcast_to([128, 64]), op=ALU.mult)
                return e.tensor_copy(
                    out=sb["vgstage"][:, tsub * 1040: (tsub + 1) * 1040]
                        .rearrange("p (h c) -> p h c", c=65)[:, :, 64:65],
                    in_=gt[:].rearrange("p (h c) -> p h c", c=1))
            p.O("vector", vevac, inc="vd")
            p.W("tensor", "vd")

    for ch in range(NB):
        p.W("sync", "vd")
        for i in range(8):
            p.O("sync", lambda e, ch=ch, i=i: e.dma_start(
                out=sb["xchunk"][:, i * 512:(i + 1) * 512],
                in_=g["xb"][i * 128:(i + 1) * 128, ch * 512:(ch + 1) * 512]), inc="ld", k=16)
        p.W("vector", "ld")
        emit_ln_chunk(lambda i: sb["xchunk"][:, i * 512:(i + 1) * 512],
                      lambda i: sb["xnb"][:, i * 512:(i + 1) * 512])
        emit_kproj("wk16", "kstage",
                   spill=lambda i, ch=ch: g["hK"].rearrange("p (d t) -> p d t", d=8)[:, i, ch * 512:(ch + 1) * 512])
        emit_headnorm2("kstage", 10)
        p.Wv("vector", "vd", p.ct("vd"))
        p.O("vector", lambda e: e.tensor_tensor(out=sb["nrm"][0:16, 1:2], in0=sb["nrm"][0:16, 1:2],
            in1=sb["sv"][0:16, 10:11], op=ALU.max), inc="vd")
        p.Wv("vector", "vd", p.ct("vd"))
        emit_ph_proj()
        emit_gate_chain(lambda t, ch=ch: sb["lngate_tm"][:, (ch * 4 + t) * 16:(ch * 4 + t + 1) * 16])
        p.W("sync", "vd")
        p.O("sync", lambda e, ch=ch: e.dma_start(
            out=g["hGT"][:, ch * 512:(ch + 1) * 512], in_=sb["gatec"][:]), inc="sp", k=16)
        emit_vproj_gated(lambda t, ch=ch: sb["lngate_tm"][:, (ch * 4 + t) * 16:(ch * 4 + t + 1) * 16])
        p.W("sync", "vd")
        for tsub in range(4):
            p.O("sync", lambda e, ch=ch, tsub=tsub: e.dma_start(
                out=g["hVg"][:, (ch * 4 + tsub) * 1040:(ch * 4 + tsub + 1) * 1040],
                in_=sb["vgstage"][:, tsub * 1040:(tsub + 1) * 1040]), inc="sp", k=16)

    if stage != "p1":
        for s6 in range(NSLOT):
            nb = BSIDE[s6] // 128
            p.W("sync", "vd")
            for i in range(8):
                p.O("sync", lambda e, s6=s6, i=i: e.dma_start(
                    out=sb["xchunk"][:, i * 512:(i + 1) * 512],
                    in_=g["xg"][i * 128:(i + 1) * 128, s6 * 512:(s6 + 1) * 512]), inc="ld", k=16)
            p.W("vector", "ld")
            emit_ln_chunk(lambda i: sb["xchunk"][:, i * 512:(i + 1) * 512],
                          lambda i: sb["xnb"][:, i * 512:(i + 1) * 512])
            p.W("sync", "vd")
            for i in range(8):
                p.O("sync", lambda e, s6=s6, i=i: e.dma_start(
                    out=g["hXN"][:, s6 * 4096 + i * 512: s6 * 4096 + (i + 1) * 512],
                    in_=sb["xnb"][:, i * 512:(i + 1) * 512]), inc="sp", k=16)
            emit_kproj("wk16", "kstage")
            emit_headnorm2("kstage", 12)
            p.Wv("vector", "vd", p.ct("vd"))
            p.O("vector", lambda e: e.tensor_tensor(out=sb["sv"][0:16, 12:13],
                in0=sb["sv"][0:16, 12:13], in1=sb["nrm"][0:16, 1:2], op=ALU.max), inc="vd")
            p.Wv("vector", "vd", p.ct("vd"))
            emit_ph_proj()
            emit_gate_chain(lambda t: sb["biasS"][:, t * 16:(t + 1) * 16])
            emit_vproj_gated(lambda t: sb["biasS"][:, t * 16:(t + 1) * 16])
            emit_kproj("wq16", "qhat")
            emit_headnorm2("qhat", 13)
            p.Wv("vector", "vd", p.ct("vd"))
            def m0c(e):
                return e.tensor_tensor(out=sb["sv"][0:16, 14:15], in0=sb["sv"][0:16, 13:14],
                                       in1=sb["sv"][0:16, 12:13], op=ALU.mult)
            p.O("vector", m0c, inc="vd")
            p.W("scalar", "vd")
            p.O("scalar", lambda e: e.activation(out=sb["sv"][0:16, 14:15],
                in_=sb["sv"][0:16, 14:15], func=AF.Sqrt), inc="sd")
            p.W("vector", "sd")
            VCH(lambda e: e.tensor_scalar(out=sb["sv"][0:16, 14:15], in0=sb["sv"][0:16, 14:15],
                                          scalar1=1.02 / 8.0, scalar2=0.05,
                                          op0=ALU.mult, op1=ALU.add),
                lambda e: e.tensor_copy(out=sb["m0b"][0:16, 0:1], in_=sb["sv"][0:16, 14:15]))
            p.W("tensor", "vd")
            p.O("tensor", lambda e: e.transpose(psK.bitcast(BF16)[0:1, 0:16],
                sb["m0b"][0:16, 0:1], sb["ident"][0:16, 0:16]), inc="td")
            p.W("vector", "td")
            VCH(lambda e: e.tensor_copy(out=sb["row"][0:1, 0:16], in_=psK.bitcast(BF16)[0:1, 0:16]),
                lambda e: e.tensor_reduce(out=sb["row"][0:1, 32:40],
                                          in_=sb["row"][0:1, 0:16].rearrange("o (a b) -> o a b", b=2),
                                          axis=AX.X, op=ALU.max),
                lambda e: e.tensor_copy(out=sb["rowb"][0:1, 0:8], in_=sb["row"][0:1, 32:40]))
            p.W("tensor", "vd")
            p.O("tensor", lambda e: e.matmul(psNA[0:128, 0:8], sb["ones_row"][:],
                sb["rowb"][0:1, 0:8], start=True, stop=True), inc="td")
            p.W("vector", "td")
            p.O("vector", lambda e: e.tensor_scalar(out=sb["m0rep"][:, 0:8], in0=psNA[0:128, 0:8],
                scalar1=-1.0, scalar2=None, op0=ALU.mult), inc="vd")

            nunit = nb + 4
            for hp in range(8):
                h0 = 2 * hp
                first = True
                u = 0
                while u < nunit:
                    take = min(2, nunit - u)
                    any_batch = any((u + j) < nb for j in range(take))
                    if any_batch:
                        p.Wv("sync", "td", p.ct("td"))
                        for j in range(take):
                            kc = u + j
                            if kc < nb:
                                for i in range(8):
                                    p.O("sync", lambda e, kc=kc, i=i, j=j: e.dma_start(
                                        out=kbuf_ap[:, (j * 8 + i) * 128:(j * 8 + i) * 128 + 128],
                                        in_=g["hK"].rearrange("p (d t) -> p d t", d=8)[:, i, kc * 128:(kc + 1) * 128]),
                                        inc="ld", k=16)
                                p.O("sync", lambda e, kc=kc, j=j: e.dma_start(
                                    out=sb["vgbuf"][:, j * 1040:(j + 1) * 1040],
                                    in_=g["hVg"][:, kc * 1040:(kc + 1) * 1040]), inc="ld", k=16)
                        p.Wv("tensor", "ld", p.ct("ld"))
                    for j in range(take):
                        kc = u + j
                        for hh in range(2):
                            h = h0 + hh
                            bank = j * 2 + hh
                            poison = (kc < nb) and PADP[s6][kc]
                            if kc < nb:
                                lhs = kbuf_ap[(h % 2) * 64:(h % 2) * 64 + 64,
                                              (j * 8 + (h // 2)) * 128:(j * 8 + (h // 2)) * 128 + 128]
                            else:
                                jj = kc - nb
                                lhs = sb["kstage"][(h % 2) * 64:(h % 2) * 64 + 64,
                                                   (h // 2) * 512 + jj * 128:(h // 2) * 512 + jj * 128 + 128]
                            rhs = sb["qhat"][(h % 2) * 64:(h % 2) * 64 + 64,
                                             (h // 2) * 512:(h // 2) * 512 + 512]
                            is_last = (j == take - 1) and (hh == 1) and not poison
                            p.O("tensor", lambda e, lhs=lhs, rhs=rhs, bank=bank, h=h,
                                       poison=poison: e.matmul(
                                psSC[:, bank * 512:(bank + 1) * 512], lhs, rhs,
                                start=True, stop=not poison,
                                tile_position=((h % 2) * 64, 0)),
                                inc="td" if is_last else None)
                            if poison:
                                is_last2 = (j == take - 1) and (hh == 1)
                                p.O("tensor", lambda e, bank=bank, s6=s6, kc=kc: e.matmul(
                                    psSC[:, bank * 512:(bank + 1) * 512],
                                    sb["ones_row"][:],
                                    sb["smrowb"][0:1, s6 * 28 + kc:s6 * 28 + kc + 1].broadcast_to([1, 512]),
                                    start=False, stop=True),
                                    inc="td" if is_last2 else None)
                    p.Wv("scalar", "td", p.ct("td"))
                    p.O("scalar", lambda e, take=take, hp=hp: e.activation(
                        out=sb["ptil"][:, 0:take * 1024],
                        in_=psSC[:, 0:take * 1024], func=AF.Exp,
                        scale=1.0 / 8.0, bias=sb["m0rep"][:, hp:hp + 1]), inc="sd")
                    has_slot = any((u + j) >= nb for j in range(take))
                    if has_slot:
                        p.Wv("vector", "sd", p.ct("sd"))
                        for j in range(take):
                            kc = u + j
                            if kc >= nb:
                                jj = kc - nb
                                for hh in range(2):
                                    bank = j * 2 + hh
                                    p.O("vector", lambda e, bank=bank, jj=jj: e.tensor_tensor(
                                        out=sb["ptil"][:, bank * 512 + jj * 128:bank * 512 + jj * 128 + 128],
                                        in0=sb["ptil"][:, bank * 512 + jj * 128:bank * 512 + jj * 128 + 128],
                                        in1=sb["tri_diag"][:, 0:128], op=ALU.mult), inc="vd")
                        p.Wv("tensor", "vd", p.ct("vd"))
                    else:
                        p.Wv("tensor", "sd", p.ct("sd"))
                    last_u = (u + take >= nunit)
                    for j in range(take):
                        kc = u + j
                        for hh in range(2):
                            h = h0 + hh
                            if kc < nb:
                                vg = sb["vgbuf"][:, j * 1040 + h * 65:j * 1040 + h * 65 + 65]
                                qoff = 0
                            else:
                                jj = kc - nb
                                vg = sb["vgstage"][:, jj * 1040 + h * 65:jj * 1040 + h * 65 + 65]
                                qoff = jj * 128
                            is_lastav = last_u and (j == take - 1) and (hh == 1)
                            p.O("tensor", lambda e, vg=vg, j=j, hh=hh, qoff=qoff,
                                       st=(first and j == 0), sp_=(last_u and j == take - 1): e.matmul(
                                psAV[0:65, hh * 512 + qoff:(hh + 1) * 512],
                                vg, sb["ptil"][:, (j * 2 + hh) * 512 + qoff:(j * 2 + hh + 1) * 512],
                                start=st, stop=sp_),
                                inc="td" if is_lastav else None)
                    first = False
                    p.Wv("scalar", "td", p.ct("td"))
                    u += take
                p.W("vector", "td")
                p.O("vector", lambda e: e.tensor_copy(out=sb["row"][0:1, 0:1024],
                    in_=psAV[64:65, 0:1024]))
                p.O("vector", lambda e: e.reciprocal(out=sb["row"][0:1, 1024:2048],
                    in_=sb["row"][0:1, 0:1024]), inc="vd")
                p.Wv("vector", "vd", p.ct("vd"))
                def pairnorm(e, hp=hp):
                    e.tensor_scalar(out=sb["rowb"][0:1, 0:512], in0=sb["row"][0:1, 1024:1536],
                                    scalar1=sb["amrow"][0:1, 2 * hp:2 * hp + 1], scalar2=None,
                                    op0=ALU.mult)
                    return e.tensor_scalar(out=sb["rowb"][0:1, 512:1024], in0=sb["row"][0:1, 1536:2048],
                                           scalar1=sb["amrow"][0:1, 2 * hp + 1:2 * hp + 2], scalar2=None,
                                           op0=ALU.mult)
                p.O("vector", pairnorm, inc="vd")
                p.W("tensor", "vd")
                for hh in range(2):
                    p.O("tensor", lambda e, hh=hh: e.matmul(
                        (psNA if hh == 0 else psNB)[0:128, 0:512], sb["ones_row"][:],
                        sb["rowb"][0:1, hh * 512:(hh + 1) * 512], start=True, stop=True),
                        inc="td" if hh == 1 else None)
                p.W("vector", "td")
                def oevac(e, hp=hp):
                    e.tensor_copy(out=sb["t0"][:], in_=psNA[0:128, 0:512])
                    e.tensor_copy(out=sb["t1"][:], in_=psNB[0:128, 0:512])
                    e.tensor_tensor(
                        out=sb["obuf"][0:64, hp * 512:(hp + 1) * 512],
                        in0=psAV[0:64, 0:512], in1=sb["t0"][0:64, :], op=ALU.mult)
                    return e.tensor_tensor(
                        out=sb["kst16"][0:64, 0:512],
                        in0=psAV[0:64, 512:1024], in1=sb["t1"][0:64, :], op=ALU.mult)
                p.O("vector", oevac, inc="vd")
                p.W("sync", "vd")
                p.O("sync", lambda e, hp=hp: e.dma_start(
                    out=sb["obuf"][64:128, hp * 512:(hp + 1) * 512],
                    in_=sb["kst16"][0:64, 0:512]), inc="ld", k=16)
                p.Wv("vector", "ld", p.ct("ld"))
                p.Wv("tensor", "ld", p.ct("ld"))
                p.W("tensor", "vd")
            if stage == "l0o" and s6 == 0:
                p.W("sync", "vd")
                p.O("sync", lambda e: e.dma_start(out=g["d_obuf"], in_=sb["obuf"][:]), inc="sp", k=16)
                p.W("sync", "sp")
                break_flag = True
            for w in range(2):
                for dt_ in range(4):
                    do = w * 4 + dt_
                    for i in range(8):
                        p.O("tensor", lambda e, do=do, dt_=dt_, i=i: e.matmul(
                            psK[:, dt_ * 512:dt_ * 512 + 512],
                            sb["wo16"][:, i * 1024 + do * 128: i * 1024 + do * 128 + 128],
                            sb["obuf"][:, i * 512:(i + 1) * 512],
                            start=(i == 0), stop=(i == 7)), inc="td" if i == 7 else None)
                p.W("vector", "td")
                for dt_ in range(4):
                    do = w * 4 + dt_
                    p.O("vector", lambda e, do=do, dt_=dt_: e.tensor_copy(
                        out=sb["kstage"][:, do * 512:(do + 1) * 512],
                        in_=psK[:, dt_ * 512:dt_ * 512 + 512]), inc="vd")
                p.W("tensor", "vd")
            p.W("sync", "vd")
            for i in range(8):
                p.O("sync", lambda e, s6=s6, i=i: e.dma_start(
                    out=g["hS0"][:, i * 3072 + s6 * 512: i * 3072 + (s6 + 1) * 512],
                    in_=sb["kstage"][:, i * 512:(i + 1) * 512]), inc="sp", k=16)


    if stage not in ("p1",):
        xchunk_i32 = sb["xchunk"][:].bitcast(I32)
        for dt_ in range(24):
            p.Wv("sync", "td", p.ct("td"))
            p.W("sync", "vd")
            for i in range(8):
                p.O("sync", lambda e, dt_=dt_, i=i: e.dma_start(
                    out=xchunk_i32[:, i * 128:(i + 1) * 128],
                    in_=g["w_l0_wg"][i * 128:(i + 1) * 128, dt_ * 128:(dt_ + 1) * 128]), inc="wld", k=16)
                p.O("sync", lambda e, dt_=dt_, i=i: e.dma_start(
                    out=xchunk_i32[:, 1024 + i * 128:1024 + (i + 1) * 128],
                    in_=g["w_l0_wu"][i * 128:(i + 1) * 128, dt_ * 128:(dt_ + 1) * 128]), inc="wld", k=16)
            p.W("vector", "wld")
            p.W("vector", "td")
            p.O("vector", lambda e: e.tensor_scalar(
                out=sb["wstr"][:, 0:2048], in0=xchunk_i32[:, 0:2048],
                scalar1=1.0, scalar2=None, op0=ALU.mult), inc="vd")
            for s6 in range(NSLOT):
                p.Wv("sync", "td", p.ct("td"))
                for i in range(8):
                    p.O("sync", lambda e, s6=s6, i=i: e.dma_start(
                        out=sb["xnb"][:, i * 512:(i + 1) * 512],
                        in_=g["hXN"][:, s6 * 4096 + i * 512: s6 * 4096 + (i + 1) * 512]), inc="ld", k=16)
                p.Wv("tensor", "ld", p.ct("ld"))
                p.Wv("tensor", "vd", p.ct("vd"))
                for i in range(8):
                    p.O("tensor", lambda e, i=i: e.matmul(
                        psA[:, :], sb["wstr"][:, i * 128:(i + 1) * 128],
                        sb["xnb"][:, i * 512:(i + 1) * 512],
                        start=(i == 0), stop=(i == 7)))
                for i in range(8):
                    p.O("tensor", lambda e, i=i: e.matmul(
                        psB[:, :], sb["wstr"][:, 1024 + i * 128:1024 + (i + 1) * 128],
                        sb["xnb"][:, i * 512:(i + 1) * 512],
                        start=(i == 0), stop=(i == 7)), inc="td" if i == 7 else None)
                p.W("scalar", "td")
                p.O("scalar", lambda e: e.activation(out=sb["t0"][:], in_=psA[:, :],
                    func=AF.Silu, scale=1.0 / 127.0), inc="sd")
                p.W("vector", "sd")
                p.O("vector", lambda e: e.tensor_tensor(out=sb["kst16"][:, 0:512],
                    in0=psB[:, :], in1=sb["t0"][:], op=ALU.mult), inc="vd")
                p.W("sync", "vd")
                p.O("sync", lambda e, dt_=dt_, s6=s6: e.dma_start(
                    out=g["hH0"][:, (dt_ * 6 + s6) * 512:(dt_ * 6 + s6 + 1) * 512],
                    in_=sb["kst16"][:, 0:512]), inc="sp", k=16)
        for s6 in range(NSLOT):
            for wv_ in range(2):
                for dt_ in range(24):
                    p.Wv("sync", "td", p.ct("td"))
                    p.W("sync", "vd")
                    p.O("sync", lambda e, wv_=wv_, dt_=dt_: e.dma_start(
                        out=xchunk_i32[:, 0:512],
                        in_=g["w_l0_wd"][dt_ * 128:(dt_ + 1) * 128, wv_ * 512:(wv_ + 1) * 512]),
                        inc="wld", k=16)
                    p.O("sync", lambda e, s6=s6, dt_=dt_: e.dma_start(
                        out=sb["kst16"][:, 512:1024].bitcast(BF16) if False else sb["hti"][:, 0:512],
                        in_=g["hH0"][:, (dt_ * 6 + s6) * 512:(dt_ * 6 + s6 + 1) * 512]), inc="ld", k=16)
                    p.W("vector", "wld")
                    p.W("vector", "td")
                    p.O("vector", lambda e: e.tensor_scalar(
                        out=sb["wstr"][:, 0:512], in0=xchunk_i32[:, 0:512],
                        scalar1=1.0, scalar2=None, op0=ALU.mult), inc="vd")
                    p.Wv("tensor", "vd", p.ct("vd"))
                    p.Wv("tensor", "ld", p.ct("ld"))
                    for q4 in range(4):
                        p.O("tensor", lambda e, q4=q4, dt_=dt_: e.matmul(
                            psK[:, q4 * 512:(q4 + 1) * 512],
                            sb["wstr"][:, q4 * 128:(q4 + 1) * 128],
                            sb["hti"][:, 0:512],
                            start=(dt_ == 0), stop=(dt_ == 23)),
                            inc="td" if q4 == 3 else None)
                p.W("scalar", "td")
                for q4 in range(4):
                    p.O("scalar", lambda e, wv_=wv_, q4=q4: e.activation(
                        out=sb["qhat"][:, (wv_ * 4 + q4) * 512:(wv_ * 4 + q4 + 1) * 512],
                        in_=psK[:, q4 * 512:(q4 + 1) * 512],
                        func=AF.Silu, scale=1.0 / (127.0 * 127.0)),
                        inc="sd" if q4 == 3 else None)
                p.W("tensor", "sd")
            p.Wv("sync", "td", p.ct("td"))
            p.W("sync", "sd"); p.W("sync", "vd")
            for i in range(8):
                p.O("sync", lambda e, s6=s6, i=i: e.dma_start(
                    out=sb["xchunk"][:, i * 512:(i + 1) * 512],
                    in_=g["xg"][i * 128:(i + 1) * 128, s6 * 512:(s6 + 1) * 512]), inc="ld", k=16)
                p.O("sync", lambda e, s6=s6, i=i: e.dma_start(
                    out=sb["kstage"][:, i * 512:(i + 1) * 512],
                    in_=g["hS0"][:, i * 3072 + s6 * 512: i * 3072 + (s6 + 1) * 512]), inc="ld", k=16)
            p.W("vector", "ld")
            for i in range(8):
                def comb(e, s6=s6, i=i):
                    e.tensor_tensor(out=sb["t0"][:],
                                    in0=sb["kstage"][:, i * 512:(i + 1) * 512],
                                    in1=sb["qhat"][:, i * 512:(i + 1) * 512], op=ALU.mult)
                    return e.tensor_tensor(out=sb["t1"][:], in0=sb["t0"][:],
                                           in1=sb["xchunk"][:, i * 512:(i + 1) * 512], op=ALU.add)
                p.O("vector", comb, inc="vd")
                p.W("sync", "vd")
                p.O("sync", lambda e, s6=s6, i=i: e.dma_start(
                    out=g["hX1"][:, i * 3072 + s6 * 512: i * 3072 + (s6 + 1) * 512],
                    in_=sb["t1"][:]), inc="sp", k=16)
                p.Wv("vector", "sp", p.ct("sp"))


    if stage not in ("p1", "l0a", "l0o", "x1"):
        OWNED6 = [1, 2, 4, 5]
        k1e = sb["xn_slots"]
        def q1v(h):
            return sb["qhat"] if h < 8 else sb["kstage"]
        def habs1_pair(j):
            nm = "wk16" if 2 * j < 16 else "wv16"
            off = ((2 * j) % 16) * 512
            return sb[nm][:].bitcast(FP16)[:, off:off + 1024]
        HB2 = []
        for nm_, ntile_ in (("xn_slots", 24), ("wk16", 16), ("wv16", 16), ("wq16", 16),
                            ("wo16", 16), ("xchunk", 16), ("xnb", 8), ("v1g", 12), ("vgstage", 8)):
            for j_ in range(ntile_):
                HB2.append((nm_, j_ * 512))
        def habs2_t(t):
            nm, off = HB2[t]
            return sb[nm][:].bitcast(FP16)[:, off:off + 512]
        def habs2_pair(j):
            nm, off = HB2[2 * j]
            return sb[nm][:].bitcast(FP16)[:, off:off + 1024]
        def q2t_t(t):
            nm, off = HB2[t]
            return sb[nm][:].bitcast(BF16)[:, off:off + 512]
        def q1t_v(d):
            nm = ["obuf", "ptil", "qhat", "kstage"][d // 2]
            return sb[nm][:, (d % 2) * 2048:(d % 2) * 2048 + 2048]
        def _kp1f(N):
            qv = np.float32(np.float32(199.0) / np.float32(200.0))
            pp = np.float32(qv * np.float32(np.float32(N) - np.float32(1.0)))
            return float(np.floor(pp)) + 1.0
        KP1_1 = _kp1f(4 * 512 * 1024 * nranks)
        KP1_2 = _kp1f(4 * 512 * 4096 * nranks)
        MAGIC = 12582912.0

        for i in range(8):
            p.O("sync", lambda e, i=i: e.dma_start(out=sb["qnw8"][:, i:i + 1],
                in_=g["v_qnw"][i * 128:(i + 1) * 128, :]), inc="ld", k=16)
            p.O("sync", lambda e, i=i: e.dma_start(out=sb["lng8"][:, i:i + 1],
                in_=g["v_lng"][i * 128:(i + 1) * 128, :]), inc="ld", k=16)
            p.O("sync", lambda e, i=i: e.dma_start(out=sb["lnb8"][:, i:i + 1],
                in_=g["v_lnb"][i * 128:(i + 1) * 128, :]), inc="ld", k=16)
        for i in range(2):
            p.O("sync", lambda e, i=i: e.dma_start(out=sb["knw2"][:, i:i + 1],
                in_=g["v_knw"][i * 128:(i + 1) * 128, :]), inc="ld", k=16)
        cast_weight("w_l1_wq", "wq16", 1024)
        cast_weight("w_l1_wo", "wo16", 1024)
        for wname, vbase in (("w_l1_wk", 0), ("w_l1_wv", 2048)):
            p.W("sync", "vd")
            for i in range(8):
                p.O("sync", lambda e, i=i, w=wname: e.dma_start(
                    out=sb["xchunk"][:, i * 256:(i + 1) * 256],
                    in_=g[w][i * 128:(i + 1) * 128, 0:256]), inc="wld", k=16)
            p.W("vector", "wld")
            p.W("vector", "td")
            p.O("vector", lambda e, vbase=vbase: e.tensor_copy(
                out=sb["vgstage"][:, vbase:vbase + 2048],
                in_=sb["xchunk"][:, 0:2048]), inc="vd")
        p.W("gpsimd", "vd"); p.W("gpsimd", "td")
        p.O("gpsimd", lambda e: e.memset(k1e[64:96, :], 0.0), inc="gd")
        p.O("gpsimd", lambda e: e.memset(k1e[96:128, :], 0.0), inc="gd")
        p.O("gpsimd", lambda e: e.memset(sb["onesf"][:], 1.0), inc="gd")
        p.O("gpsimd", lambda e: e.memset(sb["onescf"][:], 1.0), inc="gd")
        p.O("gpsimd", lambda e: e.memset(sb["sv"][:, 9:10], 0.0), inc="gd")
        p.W("sync", "gd")
        for gg in range(4):
            p.O("sync", lambda e, gg=gg: e.dma_start(
                out=k1e[64:76, gg * 3072:(gg + 1) * 3072], in_=g["c_kspec"]), inc="ld", k=16)
        p.W("gpsimd", "ld")
        p.O("gpsimd", lambda e: e.memset(k1e[96:97, :], 1.0), inc="gd")

        OWNIX = {1: 0, 2: 1, 4: 2, 5: 3}
        for s6 in range(NSLOT):
            p.W("sync", "vd")
            for i in range(8):
                p.O("sync", lambda e, s6=s6, i=i: e.dma_start(
                    out=sb["xchunk"][:, i * 512:(i + 1) * 512],
                    in_=g["hX1"][:, i * 3072 + s6 * 512: i * 3072 + (s6 + 1) * 512]), inc="ld", k=16)
            p.W("vector", "ld")
            emit_ln_chunk(lambda i: sb["xchunk"][:, i * 512:(i + 1) * 512],
                          lambda i: sb["xnb"][:, i * 512:(i + 1) * 512])
            own = s6 in OWNIX
            if own:
                def stash(e):
                    e.tensor_copy(out=sb["gr0"][0:1, :], in_=sb["row"][0:1, 0:512])
                    e.tensor_copy(out=sb["gr1"][0:1, :], in_=sb["row"][0:1, 1024:1536])
                    return e.tensor_copy(out=sb["gatec"][0:1, :], in_=sb["row"][0:1, 512:1024])
                p.O("vector", stash, inc="vd")
            p.W("sync", "vd")
            for i in range(8):
                p.O("sync", lambda e, s6=s6, i=i: e.dma_start(
                    out=g["hXN"][:, s6 * 4096 + i * 512: s6 * 4096 + (i + 1) * 512],
                    in_=sb["xnb"][:, i * 512:(i + 1) * 512]), inc="sp", k=16)
            p.W("tensor", "vd")
            for dt_ in range(2):
                for i in range(8):
                    p.O("tensor", lambda e, dt_=dt_, i=i: e.matmul(
                        psK[:, dt_ * 512:(dt_ + 1) * 512],
                        sb["vgstage"][:, i * 256 + dt_ * 128: i * 256 + dt_ * 128 + 128],
                        sb["xnb"][:, i * 512:(i + 1) * 512],
                        start=(i == 0), stop=(i == 7)), inc="td" if (dt_ == 1 and i == 7) else None)
            p.W("vector", "td")
            def kev(e):
                e.tensor_copy(out=sb["t0"][:], in_=psK[:, 0:512])
                e.tensor_copy(out=sb["t1"][:], in_=psK[:, 512:1024])
                e.tensor_tensor(out=sb["ptil"][:, 0:512], in0=sb["t0"][:], in1=sb["t0"][:], op=ALU.mult)
                return e.tensor_tensor(out=sb["ptil"][:, 512:1024], in0=sb["t1"][:], in1=sb["t1"][:], op=ALU.mult)
            p.O("vector", kev, inc="vd")
            p.W("tensor", "vd")
            for jj in range(2):
                p.O("tensor", lambda e, jj=jj: e.matmul(psA[0:1, :], sb["ones_col"][:],
                    sb["ptil"][:, jj * 512:(jj + 1) * 512], start=(jj == 0), stop=(jj == 1)),
                    inc="td" if jj == 1 else None)
            p.W("vector", "td")
            p.O("vector", lambda e: e.tensor_scalar(out=sb["row"][0:1, 0:512], in0=psA[0:1, :],
                scalar1=1.0 / 256.0, scalar2=None, op0=ALU.mult), inc="vd")
            p.W("scalar", "vd")
            p.O("scalar", lambda e: e.activation(out=sb["row"][0:1, 512:1024],
                in_=sb["row"][0:1, 0:512], func=AF.Sqrt, bias=sb["epsc"][0:1, 2:3]), inc="sd")
            p.W("vector", "sd")
            p.O("vector", lambda e: e.reciprocal(out=sb["row"][0:1, 0:512],
                in_=sb["row"][0:1, 512:1024]), inc="vd")
            p.Wv("vector", "vd", p.ct("vd"))
            def krow(e):
                e.tensor_copy(out=sb["rowb"][0:1, 0:512], in_=sb["row"][0:1, 0:512])
                e.tensor_copy(out=sb["row"][0:1, 1024:1536], in_=sb["rowb"][0:1, 0:512])
                return e.tensor_tensor(out=sb["rowb"][0:1, 512:1024], in0=sb["row"][0:1, 0:512],
                                       in1=sb["row"][0:1, 1024:1536], op=ALU.subtract)
            p.O("vector", krow, inc="vd")
            p.W("tensor", "vd")
            p.O("tensor", lambda e: e.matmul(psB[0:128, :], sb["ones_row"][:],
                sb["rowb"][0:1, 0:512], start=True, stop=False))
            p.O("tensor", lambda e: e.matmul(psB[0:128, :], sb["ones_row"][:],
                sb["rowb"][0:1, 512:1024], start=False, stop=True), inc="td")
            p.W("vector", "td")
            def kscale(e, s6=s6):
                e.tensor_copy(out=sb["t2"][:], in_=psB[0:128, :])
                for gg in range(4):
                    b0 = (gg % 2) * 64
                    src = sb["t0"] if gg < 2 else sb["t1"]
                    e.scalar_tensor_tensor(
                        out=k1e[0:64, gg * 3072 + s6 * 512: gg * 3072 + (s6 + 1) * 512],
                        in0=src[b0:b0 + 64, :],
                        scalar=sb["knw2"][b0:b0 + 64, gg // 2:gg // 2 + 1],
                        in1=sb["t2"][b0:b0 + 64, :], op0=ALU.mult, op1=ALU.mult)
                return e.tensor_copy(out=sb["sv"][0:1, 15:16], in_=sb["sv"][0:1, 15:16])
            p.O("vector", kscale, inc="vd")
            p.W("tensor", "vd")
            for tsub in range(4):
                for i in range(8):
                    p.O("tensor", lambda e, tsub=tsub, i=i: e.matmul(
                        psK[:, 0:256],
                        sb["xnb"][:, i * 512 + tsub * 128: i * 512 + tsub * 128 + 128],
                        sb["vgstage"][:, 2048 + i * 256: 2048 + (i + 1) * 256],
                        start=(i == 0), stop=(i == 7)), inc="td" if i == 7 else None)
                p.W("vector", "td")
                def v1evac(e, s6=s6, tsub=tsub):
                    ks = s6 * 4 + tsub
                    e.tensor_copy(out=sb["v1g"][:, ks * 260:ks * 260 + 260]
                                  .rearrange("p (h c) -> p h c", c=65)[:, :, 0:64],
                                  in_=psK[:, 0:256].rearrange("p (h c) -> p h c", c=64))
                    return e.memset(sb["v1g"][:, ks * 260:ks * 260 + 260]
                                    .rearrange("p (h c) -> p h c", c=65)[:, :, 64:65], 1.0)
                p.O("vector", v1evac, inc="vd")
                p.W("tensor", "vd")
            if own:
                oix = OWNIX[s6]
                for i in range(8):
                    p.O("vector", lambda e, i=i: e.tensor_tensor(
                        out=sb["ptil"][:, i * 512:(i + 1) * 512],
                        in0=sb["xnb"][:, i * 512:(i + 1) * 512],
                        in1=sb["xnb"][:, i * 512:(i + 1) * 512], op=ALU.mult),
                        inc="vd" if i == 7 else None)
                p.W("tensor", "vd")
                for i in range(8):
                    p.O("tensor", lambda e, i=i: e.matmul(psA[0:1, :], sb["ones_col"][:],
                        sb["xnb"][:, i * 512:(i + 1) * 512], start=(i == 0), stop=(i == 7)))
                for i in range(8):
                    p.O("tensor", lambda e, i=i: e.matmul(psB[0:1, :], sb["ones_col"][:],
                        sb["ptil"][:, i * 512:(i + 1) * 512], start=(i == 0), stop=(i == 7)),
                        inc="td" if i == 7 else None)
                p.W("vector", "td")
                def istats(e):
                    e.tensor_scalar(out=sb["row"][0:1, 0:512], in0=psA[0:1, :],
                                    scalar1=1.0 / 1024.0, scalar2=None, op0=ALU.mult)
                    e.tensor_scalar(out=sb["row"][0:1, 512:1024], in0=psB[0:1, :],
                                    scalar1=1.0 / 1024.0, scalar2=None, op0=ALU.mult)
                    e.tensor_tensor(out=sb["row"][0:1, 1024:1536], in0=sb["row"][0:1, 0:512],
                                    in1=sb["row"][0:1, 0:512], op=ALU.mult)
                    return e.tensor_tensor(out=sb["row"][0:1, 512:1024],
                                           in0=sb["row"][0:1, 512:1024],
                                           in1=sb["row"][0:1, 1024:1536], op=ALU.subtract)
                p.O("vector", istats, inc="vd")
                p.W("scalar", "vd")
                p.O("scalar", lambda e: e.activation(out=sb["row"][0:1, 1024:1536],
                    in_=sb["row"][0:1, 512:1024], func=AF.Sqrt, bias=sb["epsc"][0:1, 0:1]), inc="sd")
                p.W("vector", "sd")
                p.O("vector", lambda e: e.reciprocal(out=sb["row"][0:1, 512:1024],
                    in_=sb["row"][0:1, 1024:1536]), inc="vd")
                p.Wv("vector", "vd", p.ct("vd"))
                def irows(e):
                    e.tensor_tensor(out=sb["row"][0:1, 1536:2048], in0=sb["row"][0:1, 0:512],
                                    in1=sb["gr1"][0:1, :], op=ALU.mult)
                    e.tensor_tensor(out=sb["row"][0:1, 0:512], in0=sb["row"][0:1, 1536:2048],
                                    in1=sb["gr0"][0:1, :], op=ALU.add)
                    e.tensor_tensor(out=sb["row"][0:1, 1536:2048], in0=sb["row"][0:1, 512:1024],
                                    in1=sb["gatec"][0:1, :], op=ALU.mult)
                    e.tensor_copy(out=sb["rowb"][0:1, 0:512], in_=sb["row"][0:1, 0:512])
                    e.tensor_copy(out=sb["row"][0:1, 1024:1536], in_=sb["rowb"][0:1, 0:512])
                    e.tensor_tensor(out=sb["rowb"][0:1, 512:1024], in0=sb["row"][0:1, 0:512],
                                    in1=sb["row"][0:1, 1024:1536], op=ALU.subtract)
                    e.tensor_copy(out=sb["rowb"][0:1, 1024:1536], in_=sb["row"][0:1, 1536:2048])
                    e.tensor_copy(out=sb["row"][0:1, 1024:1536], in_=sb["rowb"][0:1, 1024:1536])
                    return e.tensor_tensor(out=sb["rowb"][0:1, 1536:2048], in0=sb["row"][0:1, 1536:2048],
                                           in1=sb["row"][0:1, 1024:1536], op=ALU.subtract)
                p.O("vector", irows, inc="vd")
                p.W("tensor", "vd")
                for jj in range(2):
                    p.O("tensor", lambda e, jj=jj: e.matmul((psA if jj == 0 else psB)[0:128, :],
                        sb["ones_row"][:], sb["rowb"][0:1, jj * 1024:jj * 1024 + 512],
                        start=True, stop=False))
                    p.O("tensor", lambda e, jj=jj: e.matmul((psA if jj == 0 else psB)[0:128, :],
                        sb["ones_row"][:], sb["rowb"][0:1, jj * 1024 + 512:jj * 1024 + 1024],
                        start=False, stop=True), inc="td" if jj == 1 else None)
                p.W("vector", "td")
                for i in range(8):
                    def hh1(e, oix=oix, i=i):
                        e.tensor_tensor(out=sb["t0"][:], in0=sb["xchunk"][:, i * 512:(i + 1) * 512],
                                        in1=psA[0:128, :], op=ALU.subtract)
                        e.tensor_tensor(out=sb["t1"][:], in0=sb["t0"][:], in1=psB[0:128, :],
                                        op=ALU.mult)
                        e.tensor_scalar(out=sb["t0"][:], in0=sb["t1"][:],
                                        scalar1=sb["lng8"][:, i:i + 1],
                                        scalar2=sb["lnb8"][:, i:i + 1],
                                        op0=ALU.mult, op1=ALU.add)
                        e.tensor_copy(out=sb["hab_a"][:, 0:512], in_=sb["t0"][:])
                        e.tensor_scalar(out=sb["hab_b"][:, 0:512], in0=sb["hab_a"][:, 0:512],
                                        scalar1=-1.0, scalar2=None, op0=ALU.mult)
                        t = oix * 8 + i
                        nm = "wk16" if t < 16 else "wv16"
                        return e.tensor_tensor(
                            out=sb[nm][:].bitcast(FP16)[:, (t % 16) * 512:(t % 16) * 512 + 512],
                            in0=sb["hab_a"][:, 0:512], in1=sb["hab_b"][:, 0:512], op=ALU.max)
                    p.O("vector", hh1, inc="vd")
                    p.W("sync", "vd")
                    p.O("sync", lambda e, oix=oix, i=i: e.dma_start(
                        out=g["hH1"][:, (oix * 8 + i) * 512:(oix * 8 + i + 1) * 512],
                        in_=sb["t0"][:]), inc="sp", k=16)
                    p.Wv("vector", "sp", p.ct("sp"))

        def emit_quantile(qq, ngrp, pair_fn, kp1, c_vlo, c_dlt, c_mid, c_sel, c_scr):
            sv = sb["sv"]
            def count_round(rr):
                td0 = p.ct("td")
                for j in range(ngrp):
                    ind = (sb["t0"] if j % 2 == 0 else sb["t1"])[:].bitcast(BF16)[:, 0:1024]
                    if j >= 2:
                        p.Wv("vector", "td", td0 + j - 1)
                    p.O("vector", lambda e, j=j, ind=ind: e.tensor_scalar(
                        out=ind, in0=pair_fn(j), scalar1=sv[:, c_mid:c_mid + 1],
                        scalar2=None, op0=ALU.is_le), inc="vd")
                    p.Wv("tensor", "vd", p.ct("vd"))
                    for hh in range(2):
                        p.O("tensor", lambda e, j=j, hh=hh, ind=ind, ngrp=ngrp: e.matmul(
                            psALL[0:1, 0:512], sb["ones_col"][:], ind[:, hh * 512:(hh + 1) * 512],
                            start=(j == 0 and hh == 0), stop=(j == ngrp - 1 and hh == 1)),
                            inc="td" if hh == 1 else None)
                p.W("vector", "td")
                p.O("vector", lambda e: e.tensor_reduce(out=sb["row"][0:1, 64:65],
                    in_=psALL[0:1, 0:512], axis=AX.X, op=ALU.add), inc="vd")
                if nranks > 1:
                    p.W("gpsimd", "vd")
                    p.O("gpsimd", lambda e, rr=rr: e.dma_start(out=g[f"cc{qq}i{rr}"],
                        in_=sb["row"][0:1, 64:65]), inc="gd", k=16)
                    p.W("gpsimd", "gd")
                    p.O("gpsimd", lambda e, rr=rr: e.collective_compute(
                        "AllReduce", ALU.add, replica_groups=[list(range(nranks))],
                        ins=[g[f"cc{qq}i{rr}"]], outs=[g[f"cc{qq}o{rr}"]]), inc="cc")
                    p.W("gpsimd", "cc")
                    p.O("gpsimd", lambda e, rr=rr: e.dma_start(
                        out=sb["row"][0:1, 66:67], in_=g[f"cc{qq}o{rr}"]), inc="gd", k=16)
                    p.W("tensor", "gd")
                else:
                    p.O("vector", lambda e: e.tensor_copy(out=sb["row"][0:1, 66:67],
                        in_=sb["row"][0:1, 64:65]), inc="vd")
                    p.W("tensor", "vd")
                p.O("tensor", lambda e: e.matmul(psALL[0:128, 512:513], sb["onesf"][:],
                    sb["row"][0:1, 66:67], start=True, stop=True), inc="td")
                p.W("vector", "td")
                p.O("vector", lambda e, kp1=kp1: e.tensor_scalar(out=sv[:, c_sel:c_sel + 1],
                    in0=psALL[0:128, 512:513], scalar1=float(kp1), scalar2=None,
                    op0=ALU.is_ge), inc="vd")
            p.W("gpsimd", "vd"); p.W("gpsimd", "td")
            p.O("gpsimd", lambda e: e.memset(sv[:, c_vlo:c_vlo + 1], 2.0 ** -14), inc="gd")
            p.W("vector", "gd")
            rr = 0
            for sk in (16, 8, 4, 2, 1):
                VCH(lambda e, sk=sk: e.tensor_scalar(out=sv[:, c_mid:c_mid + 1],
                    in0=sv[:, c_vlo:c_vlo + 1], scalar1=float(2.0 ** sk), scalar2=None,
                    op0=ALU.mult))
                count_round(rr); rr += 1
                p.Wv("vector", "vd", p.ct("vd"))
                VCH(lambda e, sk=sk: e.tensor_scalar(out=sv[:, c_scr:c_scr + 1],
                        in0=sv[:, c_sel:c_sel + 1],
                        scalar1=float(1.0 - 2.0 ** sk), scalar2=float(2.0 ** sk),
                        op0=ALU.mult, op1=ALU.add),
                    lambda e: e.tensor_tensor(out=sv[:, c_mid:c_mid + 1],
                        in0=sv[:, c_vlo:c_vlo + 1], in1=sv[:, c_scr:c_scr + 1], op=ALU.mult),
                    lambda e: e.tensor_copy(out=sv[:, c_vlo:c_vlo + 1],
                        in_=sv[:, c_mid:c_mid + 1]))
            VCH(lambda e: e.tensor_scalar(out=sv[:, c_dlt:c_dlt + 1],
                in0=sv[:, c_vlo:c_vlo + 1], scalar1=0.5, scalar2=None, op0=ALU.mult))
            for _m in range(10):
                VCH(lambda e: e.tensor_tensor(out=sv[:, c_mid:c_mid + 1],
                    in0=sv[:, c_vlo:c_vlo + 1], in1=sv[:, c_dlt:c_dlt + 1], op=ALU.add))
                count_round(rr); rr += 1
                p.Wv("vector", "vd", p.ct("vd"))
                VCH(lambda e: e.tensor_scalar(out=sv[:, c_scr:c_scr + 1],
                        in0=sv[:, c_sel:c_sel + 1],
                        scalar1=-1.0, scalar2=1.0, op0=ALU.mult, op1=ALU.add),
                    lambda e: e.tensor_tensor(out=sv[:, c_scr:c_scr + 1],
                        in0=sv[:, c_scr:c_scr + 1], in1=sv[:, c_dlt:c_dlt + 1], op=ALU.mult),
                    lambda e: e.tensor_tensor(out=sv[:, c_mid:c_mid + 1],
                        in0=sv[:, c_vlo:c_vlo + 1], in1=sv[:, c_scr:c_scr + 1], op=ALU.add),
                    lambda e: e.tensor_copy(out=sv[:, c_vlo:c_vlo + 1],
                        in_=sv[:, c_mid:c_mid + 1]),
                    lambda e: e.tensor_scalar(out=sv[:, c_scr:c_scr + 1],
                        in0=sv[:, c_dlt:c_dlt + 1], scalar1=0.5, scalar2=None, op0=ALU.mult),
                    lambda e: e.tensor_copy(out=sv[:, c_dlt:c_dlt + 1],
                        in_=sv[:, c_scr:c_scr + 1]))
            VCH(lambda e: e.tensor_scalar(out=sv[:, c_scr:c_scr + 1],
                    in0=sv[:, c_dlt:c_dlt + 1], scalar1=2.0, scalar2=None, op0=ALU.mult),
                lambda e: e.tensor_tensor(out=sv[:, c_mid:c_mid + 1],
                    in0=sv[:, c_vlo:c_vlo + 1], in1=sv[:, c_scr:c_scr + 1], op=ALU.add),
                lambda e: e.tensor_scalar(out=sv[:, c_vlo:c_vlo + 1],
                    in0=sv[:, c_mid:c_mid + 1], scalar1=1e-5, scalar2=None, op0=ALU.max))

        emit_quantile(1, 16, habs1_pair, KP1_1, 0, 1, 2, 3, 4)
        p.O("vector", lambda e: e.reciprocal(out=sb["sv"][:, 4:5],
            in_=sb["sv"][:, 0:1]), inc="vd")
        p.Wv("vector", "vd", p.ct("vd"))
        p.O("vector", lambda e: e.tensor_scalar(out=sb["sv"][:, 5:6], in0=sb["sv"][:, 4:5],
            scalar1=127.0, scalar2=None, op0=ALU.mult), inc="vd")
        p.Wv("vector", "vd", p.ct("vd"))

        if stage != "l1k":

            p.W("tensor", "vd")
            for gg in range(4):
                p.O("vector", lambda e, gg=gg: e.tensor_tensor(
                    out=sb["ptil"][0:64, 0:3072], in0=k1e[0:64, gg * 3072:(gg + 1) * 3072],
                    in1=k1e[0:64, gg * 3072:(gg + 1) * 3072], op=ALU.mult), inc="vd")
                p.Wv("tensor", "vd", p.ct("vd"))
                for cc_ in range(6):
                    p.O("tensor", lambda e, cc_=cc_: e.matmul(
                        psALL[0:1, cc_ * 512:(cc_ + 1) * 512], sb["ones_col"][0:64, :],
                        sb["ptil"][0:64, cc_ * 512:(cc_ + 1) * 512], start=True, stop=True),
                        inc="td" if cc_ == 5 else None)
                p.Wv("vector", "td", p.ct("td"))
                p.O("vector", lambda e, gg=gg: e.tensor_reduce(out=sb["nrm"][0:1, gg:gg + 1],
                    in_=psALL[0:1, 0:3072], axis=AX.X, op=ALU.max), inc="vd")

            ORD = [4, 0, 1, 2, 3, 5, 6, 7]
            POS = {r: i for i, r in enumerate(ORD)}
            WAVES = [(0, [4, 0, 1]), (1536, [2, 3, 5]), (0, [6, 7])]
            for oix in range(4):
                s6 = OWNED6[oix]
                p.W("gpsimd", "vd"); p.W("gpsimd", "td")
                p.O("gpsimd", lambda e: e.memset(sb["qhat"][64:96, :], 0.0), inc="gd")
                p.O("gpsimd", lambda e: e.memset(sb["kstage"][64:96, :], 0.0), inc="gd")
                p.W("sync", "gd")
                p.O("sync", lambda e, oix=oix: e.dma_start(out=sb["qhat"][64:74, 0:4096],
                    in_=g["c_qspec"][oix * 10:oix * 10 + 10, 0:4096]), inc="ld", k=16)
                p.O("sync", lambda e, oix=oix: e.dma_start(out=sb["kstage"][64:74, 0:4096],
                    in_=g["c_qspec"][oix * 10:oix * 10 + 10, 4096:8192]), inc="ld", k=16)
                p.W("sync", "vd")
                for i in range(8):
                    p.O("sync", lambda e, s6=s6, i=i: e.dma_start(
                        out=sb["xnb"][:, i * 512:(i + 1) * 512],
                        in_=g["hXN"][:, s6 * 4096 + i * 512: s6 * 4096 + (i + 1) * 512]), inc="ld", k=16)
                p.W("tensor", "ld")
                for wv_ in range(2):
                    for d4 in range(4):
                        dt_ = wv_ * 4 + d4
                        for i in range(8):
                            p.O("tensor", lambda e, dt_=dt_, d4=d4, i=i: e.matmul(
                                psK[:, d4 * 512:(d4 + 1) * 512],
                                sb["wq16"][:, i * 1024 + dt_ * 128: i * 1024 + dt_ * 128 + 128],
                                sb["xnb"][:, i * 512:(i + 1) * 512],
                                start=(i == 0), stop=(i == 7)), inc="td" if i == 7 else None)
                    p.W("vector", "td")
                    for d4 in range(4):
                        p.O("vector", lambda e, wv_=wv_, d4=d4: e.tensor_copy(
                            out=sb["xchunk"][:, (wv_ * 4 + d4) * 512:(wv_ * 4 + d4 + 1) * 512],
                            in_=psK[:, d4 * 512:(d4 + 1) * 512]), inc="vd" if d4 == 3 else None)
                    p.W("tensor", "vd")
                for i in range(8):
                    p.O("vector", lambda e, i=i: e.tensor_tensor(
                        out=sb["ptil"][:, i * 512:(i + 1) * 512],
                        in0=sb["xchunk"][:, i * 512:(i + 1) * 512],
                        in1=sb["xchunk"][:, i * 512:(i + 1) * 512], op=ALU.mult),
                        inc="vd" if i == 7 else None)
                p.W("tensor", "vd")
                for i in range(8):
                    p.O("tensor", lambda e, i=i: e.matmul(psA[0:1, :], sb["ones_col"][:],
                        sb["ptil"][:, i * 512:(i + 1) * 512], start=(i == 0), stop=(i == 7)),
                        inc="td" if i == 7 else None)
                p.W("vector", "td")
                p.O("vector", lambda e: e.tensor_scalar(out=sb["row"][0:1, 0:512], in0=psA[0:1, :],
                    scalar1=1.0 / 1024.0, scalar2=None, op0=ALU.mult), inc="vd")
                p.W("scalar", "vd")
                p.O("scalar", lambda e: e.activation(out=sb["row"][0:1, 512:1024],
                    in_=sb["row"][0:1, 0:512], func=AF.Sqrt, bias=sb["epsc"][0:1, 2:3]), inc="sd")
                p.W("vector", "sd")
                p.O("vector", lambda e: e.reciprocal(out=sb["row"][0:1, 0:512],
                    in_=sb["row"][0:1, 512:1024]), inc="vd")
                p.Wv("vector", "vd", p.ct("vd"))
                def qrow(e):
                    e.tensor_copy(out=sb["rowb"][0:1, 0:512], in_=sb["row"][0:1, 0:512])
                    e.tensor_copy(out=sb["row"][0:1, 1024:1536], in_=sb["rowb"][0:1, 0:512])
                    return e.tensor_tensor(out=sb["rowb"][0:1, 512:1024], in0=sb["row"][0:1, 0:512],
                                           in1=sb["row"][0:1, 1024:1536], op=ALU.subtract)
                p.O("vector", qrow, inc="vd")
                p.W("tensor", "vd")
                p.O("tensor", lambda e: e.matmul(psB[0:128, :], sb["ones_row"][:],
                    sb["rowb"][0:1, 0:512], start=True, stop=False))
                p.O("tensor", lambda e: e.matmul(psB[0:128, :], sb["ones_row"][:],
                    sb["rowb"][0:1, 512:1024], start=False, stop=True), inc="td")
                p.W("vector", "td")
                p.O("vector", lambda e: e.tensor_copy(out=sb["t2"][:], in_=psB[0:128, :]), inc="vd")
                for h in range(16):
                    def qev(e, h=h):
                        b0 = (h % 2) * 64
                        return e.scalar_tensor_tensor(
                            out=q1v(h)[0:64, (h % 8) * 512:(h % 8) * 512 + 512],
                            in0=sb["xchunk"][b0:b0 + 64, (h // 2) * 512:(h // 2 + 1) * 512],
                            scalar=sb["qnw8"][b0:b0 + 64, h // 2:h // 2 + 1],
                            in1=sb["t2"][b0:b0 + 64, :], op0=ALU.mult, op1=ALU.mult)
                    p.O("vector", qev, inc="vd" if h == 15 else None)
                for h in range(16):
                    p.O("vector", lambda e, h=h: e.tensor_tensor(
                        out=sb["ptil"][0:64, (h % 2) * 512:(h % 2) * 512 + 512],
                        in0=q1v(h)[0:64, (h % 8) * 512:(h % 8) * 512 + 512],
                        in1=q1v(h)[0:64, (h % 8) * 512:(h % 8) * 512 + 512], op=ALU.mult), inc="vd")
                    p.Wv("tensor", "vd", p.ct("vd"))
                    p.O("tensor", lambda e, h=h: e.matmul(
                        psALL[0:1, 3072 + (h % 2) * 512: 3072 + (h % 2) * 512 + 512],
                        sb["ones_col"][0:64, :],
                        sb["ptil"][0:64, (h % 2) * 512:(h % 2) * 512 + 512],
                        start=True, stop=True), inc="td")
                    p.Wv("vector", "td", p.ct("td"))
                    p.O("vector", lambda e, h=h: e.tensor_reduce(
                        out=sb["row"][0:1, 80 + h:81 + h],
                        in_=psALL[0:1, 3072 + (h % 2) * 512: 3072 + (h % 2) * 512 + 512],
                        axis=AX.X, op=ALU.max), inc="vd")
                p.Wv("vector", "vd", p.ct("vd"))
                def mrowc(e):
                    kv = sb["row"][0:1, 112:128].rearrange("p (a b) -> p a b", b=4)
                    for rep in range(3):
                        e.tensor_copy(out=kv[:, :, rep:rep + 1].rearrange("p a b -> p (a b)"),
                                      in_=sb["nrm"][0:1, 0:4])
                    return e.tensor_copy(out=kv[:, :, 3:4].rearrange("p a b -> p (a b)"),
                                         in_=sb["nrm"][0:1, 0:4])
                VCH(mrowc,
                    lambda e: e.tensor_tensor(out=sb["row"][0:1, 96:112], in0=sb["row"][0:1, 80:96],
                                              in1=sb["row"][0:1, 112:128], op=ALU.mult))
                p.W("scalar", "vd")
                p.O("scalar", lambda e: e.activation(out=sb["row"][0:1, 112:128],
                    in_=sb["row"][0:1, 96:112], func=AF.Sqrt, bias=sb["epsc"][0:1, 3:4]), inc="sd")
                p.W("vector", "sd")
                p.W("vector", "ld")
                VCH(lambda e: e.tensor_scalar(out=sb["row"][0:1, 96:112],
                        in0=sb["row"][0:1, 112:128],
                        scalar1=1.02 / 8.0, scalar2=0.8, op0=ALU.mult, op1=ALU.add))
                def mfin(e):
                    for h in range(15):
                        e.tensor_scalar(out=q1v(h)[96:97, (h % 8) * 512:(h % 8) * 512 + 512],
                                        in0=sb["row"][0:1, 96 + h:97 + h].broadcast_to([1, 512]),
                                        scalar1=-8.0, scalar2=None, op0=ALU.mult)
                    return e.tensor_scalar(out=q1v(15)[96:97, 7 * 512:8 * 512],
                                           in0=sb["row"][0:1, 111:112].broadcast_to([1, 512]),
                                           scalar1=-8.0, scalar2=None, op0=ALU.mult)
                p.O("vector", mfin, inc="vd")
                p.W("tensor", "vd")
                for h in range(16):
                    gg = h // 4
                    for wv_i, (pbase, rlist) in enumerate(WAVES):
                        for ri, r in enumerate(rlist):
                            bc = (s6 - 1) * 4 + r
                            qoff = 0 if r < 4 else 128 * (r - 4)
                            qlen = 128 * (r + 1) if r < 4 else 512 - qoff
                            bank_c = pbase + ri * 512
                            p.O("tensor", lambda e, h=h, gg=gg, bc=bc, qoff=qoff, qlen=qlen, bank_c=bank_c: e.matmul(
                                psALL[:, bank_c + qoff: bank_c + qoff + qlen],
                                k1e[0:97, gg * 3072 + bc * 128: gg * 3072 + bc * 128 + 128],
                                q1v(h)[0:97, (h % 8) * 512 + qoff: (h % 8) * 512 + qoff + qlen],
                                start=True, stop=True), inc="td" if ri == len(rlist) - 1 else None)
                        p.W("scalar", "td")
                        ncols = 512 * len(rlist)
                        pt_base = 512 * POS[rlist[0]]
                        p.O("scalar", lambda e, pbase=pbase, ncols=ncols, pt_base=pt_base: e.activation(
                            out=sb["ptil"][:, pt_base:pt_base + ncols],
                            in_=psALL[:, pbase:pbase + ncols], func=AF.Exp, scale=0.125), inc="sd")
                        p.W("vector", "sd")
                        for r in rlist:
                            qoff = 0 if r < 4 else 128 * (r - 4)
                            moff = 128 * r if r < 4 else qoff
                            tri = "tri_lead" if r < 4 else "tri_diag"
                            p.O("vector", lambda e, r=r, moff=moff, tri=tri: e.tensor_tensor(
                                out=sb["ptil"][:, POS[r] * 512 + moff: POS[r] * 512 + moff + 128],
                                in0=sb["ptil"][:, POS[r] * 512 + moff: POS[r] * 512 + moff + 128],
                                in1=sb[tri][:, 0:128], op=ALU.mult),
                                inc="vd" if r == rlist[-1] else None)
                        p.W("tensor", "vd")
                        for r in rlist:
                            bc = (s6 - 1) * 4 + r
                            qoff = 0 if r < 4 else 128 * (r - 4)
                            qlen = 128 * (r + 1) if r < 4 else 512 - qoff
                            p.O("tensor", lambda e, r=r, bc=bc, gg=gg, qoff=qoff, qlen=qlen: e.matmul(
                                psALL[0:65, 3072 + qoff: 3072 + qoff + qlen],
                                sb["v1g"][:, bc * 260 + gg * 65: bc * 260 + gg * 65 + 65],
                                sb["ptil"][:, POS[r] * 512 + qoff: POS[r] * 512 + qoff + qlen],
                                start=(r == 4), stop=(r == 7)), inc="td" if r == rlist[-1] else None)
                    p.W("vector", "td")
                    p.O("vector", lambda e: e.tensor_copy(out=sb["row"][0:1, 0:512],
                        in_=psALL[64:65, 3072:3584]))
                    p.O("vector", lambda e: e.reciprocal(out=sb["row"][0:1, 512:1024],
                        in_=sb["row"][0:1, 0:512]), inc="vd")
                    p.Wv("vector", "vd", p.ct("vd"))
                    def dnorm(e):
                        e.tensor_copy(out=sb["rowb"][0:1, 0:512], in_=sb["row"][0:1, 512:1024])
                        e.tensor_copy(out=sb["row"][0:1, 1024:1536], in_=sb["rowb"][0:1, 0:512])
                        return e.tensor_tensor(out=sb["rowb"][0:1, 512:1024],
                                               in0=sb["row"][0:1, 512:1024],
                                               in1=sb["row"][0:1, 1024:1536], op=ALU.subtract)
                    p.O("vector", dnorm, inc="vd")
                    p.W("tensor", "vd")
                    p.O("tensor", lambda e: e.matmul(psALL[0:128, 3584:4096], sb["ones_row"][:],
                        sb["rowb"][0:1, 0:512], start=True, stop=False))
                    p.O("tensor", lambda e: e.matmul(psALL[0:128, 3584:4096], sb["ones_row"][:],
                        sb["rowb"][0:1, 512:1024], start=False, stop=True), inc="td")
                    p.W("vector", "td")
                    def onorm(e, h=h):
                        e.tensor_copy(out=sb["t2"][:], in_=psALL[0:128, 3584:4096])
                        return e.tensor_tensor(
                            out=sb["obuf"][(h % 2) * 64:(h % 2) * 64 + 64, (h // 2) * 512:(h // 2 + 1) * 512],
                            in0=psALL[0:64, 3072:3584], in1=sb["t2"][0:64, :], op=ALU.mult)
                    p.O("vector", onorm, inc="vd")
                    p.W("tensor", "vd")
                for wv_ in range(2):
                    for d4 in range(4):
                        dt_ = wv_ * 4 + d4
                        for i in range(8):
                            p.O("tensor", lambda e, dt_=dt_, d4=d4, i=i: e.matmul(
                                psK[:, d4 * 512:(d4 + 1) * 512],
                                sb["wo16"][:, i * 1024 + dt_ * 128: i * 1024 + dt_ * 128 + 128],
                                sb["obuf"][:, i * 512:(i + 1) * 512],
                                start=(i == 0), stop=(i == 7)), inc="td" if i == 7 else None)
                    p.W("vector", "td")
                    for d4 in range(4):
                        p.O("vector", lambda e, d4=d4: e.tensor_copy(
                            out=sb["kst16"][:, (d4 % 2) * 512:(d4 % 2) * 512 + 512],
                            in_=psK[:, d4 * 512:(d4 + 1) * 512]), inc="vd")
                        p.W("sync", "vd")
                        p.O("sync", lambda e, wv_=wv_, d4=d4, oix=oix: e.dma_start(
                            out=g["hS1"][:, (wv_ * 4 + d4) * 2048 + oix * 512:
                                         (wv_ * 4 + d4) * 2048 + oix * 512 + 512],
                            in_=sb["kst16"][:, (d4 % 2) * 512:(d4 % 2) * 512 + 512]), inc="sp", k=16)
                        if d4 >= 1:
                            p.Wv("vector", "sp", p.ct("sp") - 16)
                    p.W("tensor", "sp")

        if stage not in ("l1k", "l1s"):

            p.W("vector", "td")
            _qcts = []
            p.W("sync", "vd")
            for oix in range(4):
                for i in range(8):
                    t = oix * 8 + i
                    buf = "t1" if t % 2 == 0 else "t2"
                    if t >= 2:
                        p.Wv("sync", "vd", _qcts[t - 2])
                    p.O("sync", lambda e, t=t, buf=buf: e.dma_start(out=sb[buf][:],
                        in_=g["hH1"][:, t * 512:(t + 1) * 512]), inc="ld", k=16)
                    p.Wv("vector", "ld", p.ct("ld"))
                    def qz1(e, oix=oix, i=i, buf=buf):
                        e.tensor_scalar(out=sb["t0"][:], in0=sb[buf][:],
                                        scalar1=sb["sv"][:, 5:6], scalar2=127.0,
                                        op0=ALU.mult, op1=ALU.min)
                        e.tensor_scalar(out=sb[buf][:], in0=sb["t0"][:],
                                        scalar1=-128.0, scalar2=MAGIC, op0=ALU.max, op1=ALU.add)
                        return e.tensor_scalar(out=q1t_v(i)[:, oix * 512:(oix + 1) * 512],
                                               in0=sb[buf][:], scalar1=MAGIC, scalar2=None,
                                               op0=ALU.subtract)
                    p.O("vector", qz1, inc="vd")
                    _qcts.append(p.ct("vd"))
            p.W("sync", "vd")
            for pc in range(32):
                bslot = pc % 4
                if pc >= 4:
                    p.Wv("sync", "sd", p.ct("sd") - 3 + 0)
                p.O("sync", lambda e, pc=pc, bslot=bslot: e.dma_start(
                    out=sb["xchunk"][:, bslot * 1024:(bslot + 1) * 1024],
                    in_=g["w_l1_w1"][(pc % 8) * 128:(pc % 8) * 128 + 128,
                                     (pc // 8) * 1024:(pc // 8 + 1) * 1024]), inc="wld", k=16)
                p.Wv("scalar", "wld", p.ct("wld"))
                p.O("scalar", lambda e, bslot=bslot: e.activation(
                    out=sb["wstr"][:, 0:1024], in_=sb["xchunk"][:, bslot * 1024:(bslot + 1) * 1024],
                    func=AF.Abs, accum_out=sb["t0"][:, 0:1]), inc="sd")
                p.Wv("vector", "sd", p.ct("sd"))
                p.O("vector", lambda e, pc=pc: e.tensor_tensor(
                    out=sb["sv"][:, 10 if pc % 2 == 0 else 9:11 if pc % 2 == 0 else 10],
                    in0=sb["sv"][:, 9 if pc % 2 == 0 else 10:10 if pc % 2 == 0 else 11],
                    in1=sb["t0"][:, 0:1], op=ALU.add), inc="vd")
            p.W("tensor", "vd")
            p.O("tensor", lambda e: e.matmul(psALL[0:1, 513:514], sb["sv"][:, 9:10],
                sb["onescf"][:], start=True, stop=True), inc="td")
            p.W("vector", "td")
            p.O("vector", lambda e: e.tensor_scalar(out=sb["row"][0:1, 68:69],
                in0=psALL[0:1, 513:514], scalar1=1.0 / (4096.0 * 1024.0), scalar2=1e-5,
                op0=ALU.mult, op1=ALU.add), inc="vd")
            p.W("tensor", "vd")
            p.O("tensor", lambda e: e.matmul(psALL[0:128, 514:515], sb["onesf"][:],
                sb["row"][0:1, 68:69], start=True, stop=True), inc="td")
            p.W("vector", "td")
            p.O("vector", lambda e: e.tensor_copy(out=sb["sv"][:, 6:7],
                in_=psALL[0:128, 514:515]), inc="vd")
            p.Wv("vector", "vd", p.ct("vd"))
            p.O("vector", lambda e: e.reciprocal(out=sb["sv"][:, 7:8],
                in_=sb["sv"][:, 6:7]), inc="vd")
            p.Wv("vector", "vd", p.ct("vd"))
            VCH(lambda e: e.tensor_scalar(out=sb["sv"][:, 4:5], in0=sb["sv"][:, 0:1],
                                          scalar1=1.0 / 127.0, scalar2=None, op0=ALU.mult),
                lambda e: e.tensor_tensor(out=sb["sv"][:, 8:9], in0=sb["sv"][:, 6:7],
                                          in1=sb["sv"][:, 4:5], op=ALU.mult))
            for dot in range(32):
                p.W("sync", "vd"); p.W("sync", "td")
                for half in range(2):
                    dst = sb["t1"] if half == 0 else sb["t2"]
                    for q4 in range(4):
                        p.O("sync", lambda e, dot=dot, half=half, q4=q4, dst=dst: e.dma_start(
                            out=dst[:, q4 * 128:(q4 + 1) * 128],
                            in_=g["w_l1_w1"][(half * 4 + q4) * 128:(half * 4 + q4 + 1) * 128,
                                             dot * 128:(dot + 1) * 128]), inc="wld", k=16)
                p.W("vector", "wld")
                for half in range(2):
                    src = sb["t1"] if half == 0 else sb["t2"]
                    def tern1(e, src=src, half=half):
                        e.tensor_scalar(out=sb["t0"][:], in0=src[:], scalar1=sb["sv"][:, 7:8],
                                        scalar2=MAGIC, op0=ALU.mult, op1=ALU.add)
                        e.tensor_scalar(out=src[:], in0=sb["t0"][:], scalar1=MAGIC,
                                        scalar2=None, op0=ALU.subtract)
                        return e.tensor_scalar(out=sb["wstr"][:, half * 512:(half + 1) * 512],
                                               in0=src[:], scalar1=1.0, scalar2=-1.0,
                                               op0=ALU.min, op1=ALU.max)
                    p.O("vector", tern1, inc="vd")
                p.W("tensor", "vd")
                for tg in range(4):
                    for i in range(8):
                        p.O("tensor", lambda e, tg=tg, i=i: e.matmul(
                            psK[:, tg * 512:(tg + 1) * 512],
                            sb["wstr"][:, i * 128:(i + 1) * 128],
                            q1t_v(i)[:, tg * 512:(tg + 1) * 512],
                            start=(i == 0), stop=(i == 7)), inc="td" if i == 7 else None)
                    p.W("scalar", "td")
                    p.O("scalar", lambda e, tg=tg: e.activation(out=sb["t0"][:],
                        in_=psK[:, tg * 512:(tg + 1) * 512], func=AF.Gelu,
                        scale=sb["sv"][:, 8:9]), inc="sd")
                    p.W("vector", "sd")
                    p.O("vector", lambda e, dot=dot, tg=tg: e.tensor_scalar(
                        out=habs2_t(dot * 4 + tg), in0=sb["t0"][:], scalar1=0.0,
                        scalar2=None, op0=ALU.max), inc="vd")
                    p.W("sync", "vd")
                    p.O("sync", lambda e, dot=dot, tg=tg: e.dma_start(
                        out=g["hH2"][:, dot * 2048 + tg * 512: dot * 2048 + (tg + 1) * 512],
                        in_=sb["t0"][:]), inc="sp", k=16)
                    p.Wv("scalar", "sp", p.ct("sp"))


        if stage not in ("l1k", "l1s", "ffn1"):
            emit_quantile(2, 64, habs2_pair, KP1_2, 11, 1, 2, 3, 4)
            p.O("vector", lambda e: e.reciprocal(out=sb["sv"][:, 4:5],
                in_=sb["sv"][:, 11:12]), inc="vd")
            p.Wv("vector", "vd", p.ct("vd"))
            p.O("vector", lambda e: e.tensor_scalar(out=sb["sv"][:, 12:13], in0=sb["sv"][:, 4:5],
                scalar1=127.0, scalar2=None, op0=ALU.mult), inc="vd")
            p.Wv("vector", "vd", p.ct("vd"))
            _q2cts = []
            p.W("sync", "vd")
            for T in range(128):
                buf = "t1" if T % 2 == 0 else "t2"
                if T >= 2:
                    p.Wv("sync", "vd", _q2cts[T - 2])
                p.O("sync", lambda e, T=T, buf=buf: e.dma_start(out=sb[buf][:],
                    in_=g["hH2"][:, T * 512:(T + 1) * 512]), inc="ld", k=16)
                p.Wv("vector", "ld", p.ct("ld"))
                def qz2(e, T=T, buf=buf):
                    e.tensor_scalar(out=sb["t0"][:], in0=sb[buf][:],
                                    scalar1=sb["sv"][:, 12:13], scalar2=127.0,
                                    op0=ALU.mult, op1=ALU.min)
                    e.tensor_scalar(out=sb[buf][:], in0=sb["t0"][:],
                                    scalar1=-128.0, scalar2=MAGIC, op0=ALU.max, op1=ALU.add)
                    return e.tensor_scalar(out=q2t_t(T), in0=sb[buf][:],
                                           scalar1=MAGIC, scalar2=None, op0=ALU.subtract)
                p.O("vector", qz2, inc="vd")
                _q2cts.append(p.ct("vd"))
            p.W("gpsimd", "vd")
            p.O("gpsimd", lambda e: e.memset(sb["sv"][:, 9:10], 0.0), inc="gd")
            p.W("vector", "gd"); p.W("sync", "gd")
            for pc in range(32):
                for half in range(2):
                    dst = sb["t1"] if half == 0 else sb["t2"]
                    if pc + half > 0:
                        p.Wv("sync", "sd", p.ct("sd") - 1)
                    p.O("sync", lambda e, pc=pc, half=half, dst=dst: e.dma_start(
                        out=dst[:], in_=g["w_l1_w2"][pc * 128:(pc + 1) * 128,
                                                     half * 512:(half + 1) * 512]), inc="wld", k=16)
                    p.Wv("scalar", "wld", p.ct("wld"))
                    k_ = pc * 2 + half
                    p.O("scalar", lambda e, dst=dst, k_=k_: e.activation(
                        out=sb["wstr"][:, 0:512], in_=dst[:], func=AF.Abs,
                        accum_out=sb["obuf"][:].bitcast(F32)[:, k_:k_ + 1]), inc="sd")
            p.W("vector", "sd")
            p.O("vector", lambda e: e.tensor_reduce(out=sb["sv"][:, 9:10],
                in_=sb["obuf"][:].bitcast(F32)[:, 0:64], axis=AX.X, op=ALU.add), inc="vd")
            p.Wv("vector", "vd", p.ct("vd"))
            p.W("tensor", "vd")
            p.O("tensor", lambda e: e.matmul(psALL[0:1, 513:514], sb["sv"][:, 9:10],
                sb["onescf"][:], start=True, stop=True), inc="td")
            p.W("vector", "td")
            p.O("vector", lambda e: e.tensor_scalar(out=sb["row"][0:1, 68:69],
                in0=psALL[0:1, 513:514], scalar1=1.0 / (4096.0 * 1024.0), scalar2=1e-5,
                op0=ALU.mult, op1=ALU.add), inc="vd")
            p.W("tensor", "vd")
            p.O("tensor", lambda e: e.matmul(psALL[0:128, 514:515], sb["onesf"][:],
                sb["row"][0:1, 68:69], start=True, stop=True), inc="td")
            p.W("vector", "td")
            p.O("vector", lambda e: e.tensor_copy(out=sb["sv"][:, 13:14],
                in_=psALL[0:128, 514:515]), inc="vd")
            p.Wv("vector", "vd", p.ct("vd"))
            p.O("vector", lambda e: e.reciprocal(out=sb["sv"][:, 14:15],
                in_=sb["sv"][:, 13:14]), inc="vd")
            p.Wv("vector", "vd", p.ct("vd"))
            VCH(lambda e: e.tensor_scalar(out=sb["sv"][:, 4:5], in0=sb["sv"][:, 11:12],
                                          scalar1=1.0 / 127.0, scalar2=None, op0=ALU.mult),
                lambda e: e.tensor_tensor(out=sb["sv"][:, 15:16], in0=sb["sv"][:, 13:14],
                                          in1=sb["sv"][:, 4:5], op=ALU.mult))
            wsf = sb["wstr"][:].bitcast(F32)
            for do8 in range(8):
                for half in range(2):
                    p.W("sync", "vd"); p.W("sync", "td")
                    for grp in range(4):
                        dst = sb["t1"] if grp % 2 == 0 else sb["t2"]
                        for q4 in range(4):
                            hc = half * 16 + grp * 4 + q4
                            p.O("sync", lambda e, hc=hc, q4=q4, dst=dst, do8=do8: e.dma_start(
                                out=dst[:, q4 * 128:(q4 + 1) * 128],
                                in_=g["w_l1_w2"][hc * 128:(hc + 1) * 128,
                                                 do8 * 128:(do8 + 1) * 128]), inc="wld", k=16)
                        p.Wv("vector", "wld", p.ct("wld"))
                        def tern2(e, dst=dst, grp=grp):
                            e.tensor_scalar(out=sb["t0"][:], in0=dst[:], scalar1=sb["sv"][:, 14:15],
                                            scalar2=MAGIC, op0=ALU.mult, op1=ALU.add)
                            e.tensor_scalar(out=dst[:], in0=sb["t0"][:], scalar1=MAGIC,
                                            scalar2=None, op0=ALU.subtract)
                            return e.tensor_scalar(out=sb["wstr"][:, grp * 512:(grp + 1) * 512],
                                                   in0=dst[:], scalar1=1.0, scalar2=-1.0,
                                                   op0=ALU.min, op1=ALU.max)
                        p.O("vector", tern2, inc="vd")
                    p.W("tensor", "vd")
                    for tg in range(4):
                        for c16 in range(16):
                            hc = half * 16 + c16
                            p.O("tensor", lambda e, tg=tg, c16=c16, hc=hc, half=half: e.matmul(
                                psK[:, tg * 512:(tg + 1) * 512],
                                sb["wstr"][:, c16 * 128:(c16 + 1) * 128],
                                q2t_t(hc * 4 + tg),
                                start=(half == 0 and c16 == 0), stop=(half == 1 and c16 == 15)),
                                inc="td" if c16 == 15 else None)
                p.W("scalar", "td")
                for tg in range(4):
                    s6 = OWNED6[tg]
                    p.O("scalar", lambda e, tg=tg: e.activation(out=sb["kst16"][:, 0:512],
                        in_=psK[:, tg * 512:(tg + 1) * 512], func=AF.Silu,
                        scale=sb["sv"][:, 15:16]), inc="sd")
                    p.W("sync", "sd")
                    p.O("sync", lambda e, do8=do8, s6=s6: e.dma_start(out=sb["t0"][:],
                        in_=g["hX1"][:, do8 * 3072 + s6 * 512: do8 * 3072 + (s6 + 1) * 512]),
                        inc="ld", k=16)
                    p.O("sync", lambda e, do8=do8, tg=tg: e.dma_start(out=sb["hti"][:],
                        in_=g["hS1"][:, do8 * 2048 + tg * 512: do8 * 2048 + (tg + 1) * 512]),
                        inc="ld", k=16)
                    p.W("vector", "sd"); p.W("vector", "ld")
                    def comb(e, tg=tg):
                        e.tensor_tensor(out=wsf[:, 0:512], in0=sb["kst16"][:, 0:512],
                                        in1=sb["hti"][:], op=ALU.mult)
                        return e.tensor_tensor(out=wsf[:, 512:1024], in0=wsf[:, 0:512],
                                               in1=sb["t0"][:], op=ALU.add)
                    p.O("vector", comb, inc="vd")
                    p.W("sync", "vd")
                    if stage == "full":
                        p.O("sync", lambda e, do8=do8, tg=tg: e.dma_start(
                            out=g["y"][do8 * 128:(do8 + 1) * 128, tg * 512:(tg + 1) * 512],
                            in_=wsf[:, 512:1024]), inc="fin", k=16)
                        p.Wv("scalar", "fin", p.ct("fin"))
            if stage == "full":
                p.Wv("vector", "fin", p.ct("fin"))

    if stage == "l1k":
        p.W("sync", "vd"); p.W("sync", "td")
        p.O("sync", lambda e: e.dma_start(out=g["d_k1e"], in_=k1e[:]), inc="fin", k=16)
        p.O("sync", lambda e: e.dma_start(out=g["d_v1g"], in_=sb["v1g"][:]), inc="fin", k=16)
        p.O("sync", lambda e: e.dma_start(out=g["d_habs1"], in_=sb["wk16"][:]), inc="fin", k=16)
        p.O("sync", lambda e: e.dma_start(out=g["d_habs1b"], in_=sb["wv16"][:]), inc="fin", k=16)
        p.O("sync", lambda e: e.dma_start(out=g["d_hH1"], in_=g["hH1"]), inc="fin", k=16)
        p.O("sync", lambda e: e.dma_start(out=g["d_sv"], in_=sb["sv"][:]), inc="fin", k=16)
        p.Wv("vector", "fin", p.ct("fin"))
    if stage == "l1s":
        p.W("sync", "vd"); p.W("sync", "td"); p.W("sync", "sp")
        p.O("sync", lambda e: e.dma_start(out=g["d_hS1"], in_=g["hS1"]), inc="fin", k=16)
        p.O("sync", lambda e: e.dma_start(out=g["d_sv"], in_=sb["sv"][:]), inc="fin", k=16)
        p.Wv("vector", "fin", p.ct("fin"))
    if stage == "ffn1":
        p.W("sync", "vd"); p.W("sync", "td"); p.W("sync", "sp")
        p.O("sync", lambda e: e.dma_start(out=g["d_hH2"], in_=g["hH2"]), inc="fin", k=16)
        p.O("sync", lambda e: e.dma_start(out=g["d_sv"], in_=sb["sv"][:]), inc="fin", k=16)
        p.Wv("vector", "fin", p.ct("fin"))

    if stage == "x1":
        p.W("sync", "sp"); p.W("sync", "vd")
        p.O("sync", lambda e: e.dma_start(out=g["d_x1"], in_=g["hX1"]), inc="sp", k=16)
        p.W("sync", "sp")
        return p.finalize()

    if stage in ("l1k", "l1s", "ffn1", "full"):
        return p.finalize()

    if stage == "l0o":
        return p.finalize()
    if stage == "l0a":
        p.W("sync", "sp"); p.W("sync", "vd"); p.W("sync", "ld")
        p.O("sync", lambda e: e.dma_start(out=g["d_s0"], in_=g["hS0"]), inc="sp", k=16)
        p.W("sync", "sp")
        return p.finalize()
    if stage == "p1":
        tgt_ld = p.ct("ld"); tgt_sp = p.ct("sp"); tgt_vd = p.ct("vd")
        p.Wv("sync", "ld", tgt_ld)
        p.Wv("sync", "vd", tgt_vd)
        p.O("sync", lambda e: e.dma_start(out=g["d_gate"][:, 0:3584], in_=g["hGT"][:, 0:3584]), inc="sp", k=16)
        p.O("sync", lambda e: e.dma_start(out=g["d_lgtm"], in_=sb["lngate_tm"][:]), inc="sp", k=16)
        p.O("sync", lambda e: e.dma_start(out=g["d_nrm"], in_=sb["nrm"][:]), inc="sp", k=16)
        p.O("sync", lambda e: e.dma_start(out=g["d_sv"], in_=sb["sv"][0:16, :]), inc="sp", k=16)
        p.O("sync", lambda e: e.dma_start(out=g["d_sc"], in_=sb["sinc"][:]), inc="sp", k=16)
        p.O("sync", lambda e: e.dma_start(out=g["d_cc"], in_=sb["cosc"][:]), inc="sp", k=16)
        p.O("sync", lambda e: e.dma_start(out=g["d_hK"], in_=g["hK"]), inc="sp", k=16)
        p.O("sync", lambda e: e.dma_start(out=g["d_hVg"], in_=g["hVg"]), inc="sp", k=16)
        p.W("sync", "sp")
        return p.finalize()

    raise ValueError(stage)




OWNED_SLOTS = [1, 2, 4, 5]


def kernel(**inputs):
    maps = host_layout(inputs)
    nc = build(stage="full", dbg=(), nranks=8)
    res = run_bass_kernel_spmd(nc, maps, core_ids=list(range(8)))
    out = np.zeros((4, 4096, 1024), np.float32)
    for c in range(8):
        y = np.asarray(res.results[c]["y"], dtype=np.float32)
        b = c // 2
        groups = SLOT_GROUPS_A if c % 2 == 0 else SLOT_GROUPS_B
        for oix, s6 in enumerate(OWNED_SLOTS):
            gtok = groups[s6] * 512
            out[b, gtok:gtok + 512, :] = y[:, oix * 512:(oix + 1) * 512].T
    return out
